# version 12
# baseline (speedup 1.0000x reference)
import contextlib
import numpy as np
import concourse.bass as bass
import concourse.mybir as mybir
from concourse.bass_utils import run_bass_kernel_spmd

F32 = mybir.dt.float32
BF16 = mybir.dt.bfloat16
AF = mybir.ActivationFunctionType
ALU = mybir.AluOpType
AX = mybir.AxisListType

D = 2048
T = 2048
NSEQ = 2
NT = NSEQ * T
NH = 16
HS = 128
HF = 5632
NDC = D // 128
NHC = HF // 128
NTILE = NT // 128
NCH = NT // 512
NUNIT = NSEQ * NH
NW = HF * D
SCALE = float(1.0 / np.sqrt(HS))
SHIFT = 16.0
LNEPS = 1e-5
QEPS = 1e-8
PI = float(np.pi)
TWO_PI = float(2.0 * np.pi)
LN1E4 = float(np.log(10000.0))

ALL_PHASES = ("prep", "quant", "a0", "aq", "ak", "av", "b", "c", "d")
BUILD_PHASES = ALL_PHASES


class Ctr:
    def __init__(self, stack, nc, name):
        self.h = stack.enter_context(nc.semaphore(name))
        self.n = 0

    def inc(self, ins, k=1):
        ins.then_inc(self.h, k)
        self.n += k


class DmaQ:

    def __init__(self, stack, nc, name, k):
        self.k = k
        self.sems = [stack.enter_context(nc.semaphore(f"{name}{i}"))
                     for i in range(k)]
        self.g = 0

    def issue(self, eng, out, in_, **kw):
        g = self.g
        idx, val = g % self.k, 16 * (g // self.k + 1)
        if val > 16:
            eng.wait_ge(self.sems[idx], val - 16)
        eng.dma_start(out=out, in_=in_, **kw).then_inc(self.sems[idx], 16)
        self.g += 1
        return g

    def tok(self, g):
        return (self.sems[g % self.k], 16 * (g // self.k + 1))

    def wait(self, eng, g):
        s, v = self.tok(g)
        eng.wait_ge(s, v)

    def wait_many(self, eng, gs):
        best = {}
        for g in gs:
            idx, val = g % self.k, 16 * (g // self.k + 1)
            if best.get(idx, 0) < val:
                best[idx] = val
        for idx in sorted(best):
            eng.wait_ge(self.sems[idx], best[idx])

    def wait_all(self, eng):
        for idx in range(self.k):
            n = (self.g - idx + self.k - 1) // self.k if self.g > idx else 0
            if n > 0:
                eng.wait_ge(self.sems[idx], 16 * n)


class Ctx:
    pass


def _barrier(ctx):
    snap = [(c, c.n) for c in ctx.all_ctrs if c.n > 0]
    for sec in (ctx.block.sync, ctx.block.tensor, ctx.block.vector,
                ctx.block.scalar, ctx.block.gpsimd):
        @sec
        def _(eng, snap=snap):
            for c, v in snap:
                eng.wait_ge(c.h, v)
            for q in (ctx.QS, ctx.QG, ctx.QC, ctx.QV):
                q.wait_all(eng)


def phase_prep(ctx):
    MM, DV, AC, GP = ctx.MM, ctx.DV, ctx.AC, ctx.GP
    QS = ctx.QS
    b, f = ctx.bf, ctx.f32

    cosb = b(0, 2048)
    sinb = b(2048, 2048)
    cos4 = b(4096, 8192)
    sin4 = b(12288, 8192)
    angf = f(0, 2048)
    cbuf = f(2048, 2048)
    t_ab = f(4096, 2048)
    rowbuf = ctx.ARF[0:1, 6144:8192]
    trow = ctx.ARF[0:1, 8192:10240]

    @ctx.block.gpsimd
    def _(g):
        g.memset(ctx.onesb[:], 1.0)
        g.drain()
        g.affine_select(ctx.eye[:], ctx.onesb[:], [[-1, 128]], ALU.is_equal,
                        0.0, base=0, channel_multiplier=1)
        g.affine_select(ctx.tril[:], ctx.onesb[:], [[-1, 128]], ALU.is_ge,
                        0.0, base=0, channel_multiplier=1)
        g.affine_select(ctx.supe[:].rearrange("p (a c) -> p a c", a=64),
                        ctx.onesb[:].rearrange("p (a c) -> p a c", a=64),
                        [[-2, 64], [-256, 2]], ALU.is_equal, 0.0,
                        base=256, channel_multiplier=1)
        g.affine_select(ctx.subo[:].rearrange("p (a c) -> p a c", a=64),
                        ctx.onesb[:].rearrange("p (a c) -> p a c", a=64),
                        [[-2, 64], [-256, 2]], ALU.is_equal, 0.0,
                        base=-1, channel_multiplier=1)
        g.iota(ctx.erow[0:1, :].rearrange("p (a c) -> p a c", a=64),
               [[1, 64], [0, 2]], channel_multiplier=0,
               allow_small_or_imprecise_dtypes=True)
        g.iota(trow, [[1, 2048]], channel_multiplier=0,
               allow_small_or_imprecise_dtypes=True)
        g.memset(ctx.onesrow[:], 1.0)
        g.memset(ctx.onescol[:], 1.0)
        ins = g.memset(ctx.one11[:], 1.0)
        GP.inc(ins)

    gp_done = GP.n
    mm0, dv0, ac0 = MM.n, DV.n, AC.n
    qs0 = QS.g
    mm_ang = mm0 + 1
    mm_sp = [mm0 + 1 + bi + 1 for bi in range(4)]
    g_row = [qs0 + bi for bi in range(4)]
    dv_ang = dv0 + 1

    @ctx.block.scalar
    def _(s):
        s.wait_ge(GP.h, gp_done)
        ins = s.activation(ctx.invf[:], ctx.erow[:], AF.Exp,
                           scale=np.float32(-(LN1E4 / 64.0)))
        AC.inc(ins)

    ac_invf = AC.n

    @ctx.block.sync
    def _(sync):
        for bi, dsrc in enumerate((ctx.bq_d, ctx.bk_d, ctx.bv_d, ctx.bo_d)):
            if bi >= 1:
                sync.wait_ge(MM.h, mm_sp[bi - 1])
            QS.issue(sync, rowbuf, dsrc[0:1, :])

    @ctx.block.tensor
    def _(t):
        t.wait_ge(AC.h, ac_invf)
        t.wait_ge(GP.h, gp_done)
        for j in range(4):
            ins = t.matmul(ctx.psA[:, j * 512:(j + 1) * 512], ctx.invf[:],
                           trow[:, j * 512:(j + 1) * 512],
                           start=True, stop=True)
        MM.inc(ins)
        for bi in range(4):
            QS.wait(t, g_row[bi])
            if bi < 3:
                for dc in range(NDC):
                    ins = t.matmul(ctx.psB[:, bi * 16 + dc:bi * 16 + dc + 1],
                                   rowbuf[:, dc * 128:(dc + 1) * 128],
                                   ctx.one11[:], start=True, stop=True)
                MM.inc(ins)
            else:
                t.wait_ge(DV.h, dv_ang)
                for j in range(4):
                    ins = t.matmul(ctx.psA[:, j * 512:(j + 1) * 512],
                                   ctx.onesrow[:],
                                   rowbuf[:, j * 512:(j + 1) * 512],
                                   start=True, stop=True)
                MM.inc(ins)

    @ctx.block.vector
    def _(v):
        v.wait_ge(MM.h, mm_ang)
        v.tensor_copy(angf, ctx.psA[:, 0:2048])
        v.drain()
        steps = [(TWO_PI * (2 ** k), TWO_PI * (2 ** k))
                 for k in range(8, -1, -1)]
        steps.append((PI, TWO_PI))
        for thr, sub in steps:
            v.tensor_scalar(cbuf, angf, float(thr), None, ALU.is_ge)
            v.drain()
            v.scalar_tensor_tensor(angf, cbuf, float(-sub), angf,
                                   ALU.mult, ALU.add)
            v.drain()
        ins = v.tensor_scalar(angf, angf, PI, -PI, ALU.min, ALU.max)
        v.drain()
        DV.inc(ins)
        v.wait_ge(MM.h, mm_sp[2])
        v.tensor_scalar(ctx.bq_t[:], ctx.psB[:, 0:16], SCALE, None, ALU.mult)
        v.tensor_copy(ctx.bk_t[:], ctx.psB[:, 16:32])
        ins = v.tensor_copy(ctx.bv_t[:], ctx.psB[:, 32:48])
        DV.inc(ins)
        v.wait_ge(MM.h, mm_sp[3])
        ins = v.tensor_copy(ctx.bo_bc[:], ctx.psA[:, 0:2048])
        DV.inc(ins)

    @ctx.block.scalar
    def _(s):
        s.wait_ge(DV.h, dv_ang)
        s.activation(sinb, angf, AF.Sin)
        s.activation(t_ab, angf, AF.Abs)
        s.drain()
        ins = s.activation(cosb, t_ab, AF.Sin, scale=np.float32(-1.0),
                           bias=np.float32(PI / 2.0))
        s.drain()
        AC.inc(ins)

    ac_cos = AC.n

    @ctx.block.vector
    def _(v):
        v.wait_ge(GP.h, gp_done)
        ins = v.tensor_sub(ctx.mshuf[:], ctx.supe[:], ctx.subo[:])
        DV.inc(ins)
        v.wait_ge(AC.h, ac_cos)
        for q in range(4):
            for r in range(4):
                o = q * 2048 + r * 512
                v.tensor_copy(cos4[:, o:o + 512], cosb[:, q * 512:(q + 1) * 512])
                ins = v.tensor_copy(sin4[:, o:o + 512],
                                    sinb[:, q * 512:(q + 1) * 512])
        DV.inc(ins)


def phase_quant(ctx):
    MM, DV, AC = ctx.MM, ctx.DV, ctx.AC
    QS, QV, QG = ctx.QS, ctx.QV, ctx.QG
    b, f = ctx.bf, ctx.f32

    CW = 2816
    slab4 = f(0, 4 * CW)
    tmpA = b(20480, CW)
    tmpB = b(26112, CW)
    qbf4 = b(31744, 4 * CW)
    junkB = b(43008, 4 * CW)
    stg2 = b(54272, 4096)

    ws = [(ctx.w1_d, ctx.q1T_scr, NHC, D, True),
          (ctx.w2_d, ctx.q2T_scr, NHC, D, True),
          (ctx.w3_d, ctx.q3n_scr, 2 * NDC, CW, False)]

    def wsrc(w_d, do_T, s):
        if do_T:
            return w_d[s * 128:(s + 1) * 128, :]
        return w_d[(s // 2) * 128:(s // 2) * 128 + 128,
                   (s % 2) * CW:(s % 2) * CW + CW]

    def slot(s, ncol):
        return slab4[:, (s % 4) * CW:(s % 4) * CW + ncol]

    cnt = {0: QS.g, 1: QV.g}
    ac = AC.n
    dv = DV.n
    mm = MM.n
    qc = QG.g
    tb_par = 0
    P = {}
    for wi, (w_d, scr, nslab, ncol, do_T) in enumerate(ws):
        d = {}

        def tk(s):
            p = s % 2
            g = cnt[p]
            cnt[p] += 1
            return (p, g)

        d["g_p1"] = [tk(s) for s in range(nslab)]
        d["g_p2"] = [tk(s) for s in range(nslab)]
        d["ac_red"] = [ac + s + 1 for s in range(nslab)]
        ac += nslab
        dv += 1
        d["dv_gtot"] = dv
        dv += 1
        d["dv_thr"] = dv
        dv += 1
        d["dv_bc"] = dv
        mm += 1
        d["mm_gsum"] = mm
        mm += 1
        d["mm_rbc"] = mm
        d["dv_q"] = []
        d["dv_tb"] = []
        d["mm_tb"] = []
        d["tb_par"] = []
        for s in range(nslab):
            dv += 1
            d["dv_q"].append(dv)
            if do_T:
                row_dv, row_mm, row_par = [], [], []
                for bb in range(4):
                    mm += 1
                    row_mm.append(mm)
                    dv += 1
                    row_dv.append(dv)
                    row_par.append(tb_par)
                    tb_par += 1
                d["dv_tb"].append(row_dv)
                d["mm_tb"].append(row_mm)
                d["tb_par"].append(row_par)
        d["g_st"] = [qc + s for s in range(nslab)]
        qc += nslab
        P[wi] = d

    QMAP = {0: QS, 1: QV}

    def wait_tok(eng, t):
        QMAP[t[0]].wait(eng, t[1])

    tb_all = []
    for wi in (0, 1):
        for s in range(NHC):
            for bb in range(4):
                tb_all.append((wi, s, bb))

    @ctx.block.sync
    def _(sync):
        for wi, (w_d, scr, nslab, ncol, do_T) in enumerate(ws):
            d = P[wi]
            for s in range(0, nslab, 2):
                if s >= 4:
                    sync.wait_ge(AC.h, d["ac_red"][s - 4])
                elif wi > 0:
                    pd = P[wi - 1]
                    pn = len(pd["dv_q"])
                    sync.wait_ge(DV.h, pd["dv_q"][pn - 4 + s])
                assert (0, QS.g) == d["g_p1"][s]
                QS.issue(sync, slot(s, ncol), wsrc(w_d, do_T, s))
            for s in range(0, nslab, 2):
                if s >= 4:
                    sync.wait_ge(DV.h, d["dv_q"][s - 4])
                else:
                    sync.wait_ge(AC.h, d["ac_red"][nslab - 4 + s])
                assert (0, QS.g) == d["g_p2"][s]
                QS.issue(sync, slot(s, ncol), wsrc(w_d, do_T, s))

    @ctx.block.scalar
    def _(s_):
        for wi, (w_d, scr, nslab, ncol, do_T) in enumerate(ws):
            d = P[wi]
            for s in (1, 3):
                if wi > 0:
                    pd = P[wi - 1]
                    pn = len(pd["dv_q"])
                    s_.wait_ge(DV.h, pd["dv_q"][pn - 4 + s])
                assert (1, QV.g) == d["g_p1"][s]
                QV.issue(s_, slot(s, ncol), wsrc(w_d, do_T, s))
            for s in range(nslab):
                if s == 0 and wi > 0:
                    s_.wait_ge(DV.h, P[wi - 1]["dv_gtot"])
                wait_tok(s_, d["g_p1"][s])
                ins = s_.activation(junkB[:, (s % 4) * CW:(s % 4) * CW + ncol],
                                    slot(s, ncol), AF.Abs,
                                    accum_out=ctx.gcols[:, s:s + 1])
                AC.inc(ins)
                if s % 2 == 1 and s + 4 < nslab:
                    assert (1, QV.g) == d["g_p1"][s + 4]
                    QV.issue(s_, slot(s + 4, ncol), wsrc(w_d, do_T, s + 4))
            for s in range(1, nslab, 2):
                if s >= 4:
                    s_.wait_ge(DV.h, d["dv_q"][s - 4])
                assert (1, QV.g) == d["g_p2"][s]
                QV.issue(s_, slot(s, ncol), wsrc(w_d, do_T, s))

    @ctx.block.vector
    def _(v):
        for wi, (w_d, scr, nslab, ncol, do_T) in enumerate(ws):
            d = P[wi]
            v.wait_ge(AC.h, d["ac_red"][nslab - 1])
            ins = v.reduce_sum(ctx.gtot[:], ctx.gcols[:, 0:nslab], axis=AX.X)
            v.drain()
            DV.inc(ins)
            v.wait_ge(MM.h, d["mm_gsum"])
            ins = v.tensor_scalar(ctx.thr11[:], ctx.psB[:1, 48:49],
                                  0.5 / float(NW), 0.5 * QEPS, ALU.mult,
                                  ALU.add)
            v.drain()
            DV.inc(ins)
            v.wait_ge(MM.h, d["mm_rbc"])
            v.tensor_copy(ctx.thrbc[:], ctx.psB[:, 49:50])
            v.drain()
            ins = v.tensor_scalar(ctx.nthrbc[:], ctx.thrbc[:], -1.0, None,
                                  ALU.mult)
            v.drain()
            DV.inc(ins)
            for s in range(nslab):
                wait_tok(v, d["g_p2"][s])
                qb = qbf4[:, (s % 4) * CW:(s % 4) * CW + ncol]
                if do_T and s >= 4:
                    v.wait_ge(MM.h, d["mm_tb"][s - 4][3])
                if (not do_T) and s >= 4:
                    QG.wait(v, d["g_st"][s - 4])
                if wi > 0 and s < 4:
                    pd = P[wi - 1]
                    pn = len(pd["dv_q"])
                    if ws[wi - 1][4]:
                        v.wait_ge(MM.h, pd["mm_tb"][pn - 4 + s][3])
                    else:
                        QG.wait(v, pd["g_st"][pn - 4 + s])
                sl = slot(s, ncol)
                v.tensor_scalar(tmpA[:, 0:ncol], sl, ctx.thrbc[:, 0:1], None,
                                ALU.is_gt)
                v.tensor_scalar(tmpB[:, 0:ncol], sl, ctx.nthrbc[:, 0:1], None,
                                ALU.is_lt)
                v.drain()
                ins = v.tensor_sub(qb, tmpA[:, 0:ncol], tmpB[:, 0:ncol])
                v.drain()
                DV.inc(ins)
                if do_T:
                    stg = stg2[:, (s % 2) * 2048:(s % 2) * 2048 + 2048]
                    for bb in range(4):
                        v.wait_ge(MM.h, d["mm_tb"][s][bb])
                        if bb == 0:
                            if s >= 2:
                                QG.wait(v, d["g_st"][s - 2])
                            elif wi > 0:
                                pd = P[wi - 1]
                                QG.wait(v, pd["g_st"][len(pd["g_st"]) - 2 + s])
                        par = d["tb_par"][s][bb]
                        reg = ctx.psA if (par % 2 == 0) else ctx.psB
                        ins = v.tensor_copy(stg[:, bb * 512:(bb + 1) * 512],
                                            reg[:, 0:512])
                        DV.inc(ins)

    @ctx.block.tensor
    def _(t):
        for wi, (w_d, scr, nslab, ncol, do_T) in enumerate(ws):
            d = P[wi]
            t.wait_ge(DV.h, d["dv_gtot"])
            ins = t.matmul(ctx.psB[:1, 48:49], ctx.gtot[:], ctx.onescol[:],
                           start=True, stop=True)
            MM.inc(ins)
            t.wait_ge(DV.h, d["dv_thr"])
            ins = t.matmul(ctx.psB[:, 49:50], ctx.onesrow[:], ctx.thr11[:],
                           start=True, stop=True)
            MM.inc(ins)
            if do_T:
                for s in range(nslab):
                    t.wait_ge(DV.h, d["dv_q"][s])
                    qb = qbf4[:, (s % 4) * CW:(s % 4) * CW + ncol]
                    for bb in range(4):
                        par = d["tb_par"][s][bb]
                        if par >= 2:
                            pw, psl, pbb = tb_all[par - 2]
                            t.wait_ge(DV.h, P[pw]["dv_tb"][psl][pbb])
                        reg = ctx.psA if (par % 2 == 0) else ctx.psB
                        for j in range(4):
                            dc = bb * 4 + j
                            ins = t.matmul(reg[:, j * 128:(j + 1) * 128],
                                           qb[:, dc * 128:(dc + 1) * 128],
                                           ctx.eye[:], start=True, stop=True)
                        MM.inc(ins)

    @ctx.block.gpsimd
    def _(g_):
        for wi, (w_d, scr, nslab, ncol, do_T) in enumerate(ws):
            d = P[wi]
            for s in range(nslab):
                if do_T:
                    g_.wait_ge(DV.h, d["dv_tb"][s][3])
                    stg = stg2[:, (s % 2) * 2048:(s % 2) * 2048 + 2048]
                    assert QG.g == d["g_st"][s]
                    QG.issue(g_, scr[s], stg)
                else:
                    g_.wait_ge(DV.h, d["dv_q"][s])
                    qb = qbf4[:, (s % 4) * CW:(s % 4) * CW + ncol]
                    assert QG.g == d["g_st"][s]
                    QG.issue(g_, scr[(s // 2) * 128:(s // 2) * 128 + 128,
                                     (s % 2) * CW:(s % 2) * CW + CW], qb)


def phase_a0(ctx):
    MM, DV, AC = ctx.MM, ctx.DV, ctx.AC
    QS, QG = ctx.QS, ctx.QG
    b, f = ctx.bf, ctx.f32
    mm0, dv0, ac0 = MM.n, DV.n, AC.n
    qs0, qg0 = QS.g, QG.g

    xt2 = f(0, 2 * D)
    junk = f(2 * D, D)
    junkA = f(3 * D, D)
    xnbf = b(20480, D)
    stg = b(20480 + D, 8192)

    def dvt(i):
        return dv0 + 7 * i

    def act(i):
        return ac0 + 3 * i

    def mmt(i):
        return mm0 + 4 * i

    g_x = [qs0 + i for i in range(NTILE)]
    g_st = [qg0 + c for c in range(NCH)]

    @ctx.block.sync
    def _(sync):
        for i in range(NTILE):
            if i >= 2:
                sync.wait_ge(AC.h, act(i - 2) + 3)
            QS.issue(sync, xt2[:, (i % 2) * D:(i % 2) * D + D],
                     ctx.x_d[i * 128:(i + 1) * 128, :])

    @ctx.block.vector
    def _(v):
        for i in range(NTILE):
            xt = xt2[:, (i % 2) * D:(i % 2) * D + D]
            QS.wait(v, g_x[i])
            v.tensor_scalar(junk, xt, 1.0, None, ALU.mult, ALU.add,
                            accum_out=ctx.sx[:])
            v.drain()
            ins = v.tensor_scalar(ctx.m1[:], ctx.sx[:], 1.0 / D, None, ALU.mult)
            v.drain()
            DV.inc(ins)
            v.wait_ge(AC.h, act(i) + 1)
            v.tensor_mul(ctx.m2[:], ctx.m1[:], ctx.m1[:])
            v.drain()
            ins = v.scalar_tensor_tensor(ctx.var[:], ctx.sxx[:], 1.0 / D,
                                         ctx.m2[:], ALU.mult, ALU.subtract)
            v.drain()
            DV.inc(ins)
            v.wait_ge(AC.h, act(i) + 2)
            ins = v.scalar_tensor_tensor(ctx.nmr[:], ctx.m1[:], -1.0,
                                         ctx.rstd[:], ALU.mult, ALU.mult)
            v.drain()
            DV.inc(ins)
            j = i % 4
            for bb in range(4):
                v.wait_ge(MM.h, mmt(i) + bb + 1)
                if j == 0 and bb == 0 and i >= 4:
                    QG.wait(v, g_st[i // 4 - 1])
                reg = ctx.psA if (bb % 2 == 0) else ctx.psB
                dst = stg.rearrange("p (dc c) -> p dc c", dc=16)
                dst = dst[:, bb * 4:(bb + 1) * 4, j * 128:(j + 1) * 128]
                src = reg[:, 0:512].rearrange("p (dc c) -> p dc c", dc=4)
                ins = v.tensor_copy(dst, src)
                DV.inc(ins)

    @ctx.block.scalar
    def _(s):
        for i in range(NTILE):
            xt = xt2[:, (i % 2) * D:(i % 2) * D + D]
            QS.wait(s, g_x[i])
            if i >= 1:
                s.wait_ge(DV.h, dvt(i - 1) + 2)
            ins = s.activation(junkA, xt, AF.Square, accum_out=ctx.sxx[:])
            AC.inc(ins)
            s.wait_ge(DV.h, dvt(i) + 2)
            s.activation(ctx.tln[:], ctx.var[:], AF.Ln, bias=np.float32(LNEPS))
            s.drain()
            ins = s.activation(ctx.rstd[:], ctx.tln[:], AF.Exp,
                               scale=np.float32(-0.5))
            s.drain()
            AC.inc(ins)
            s.wait_ge(DV.h, dvt(i) + 3)
            if i >= 1:
                s.wait_ge(MM.h, mmt(i))
            ins = s.activation(xnbf, xt, AF.Identity, scale=ctx.rstd[:, 0:1],
                               bias=ctx.nmr[:, 0:1])
            s.drain()
            AC.inc(ins)

    @ctx.block.tensor
    def _(t):
        for i in range(NTILE):
            t.wait_ge(AC.h, act(i) + 3)
            for bb in range(4):
                g = 4 * i + bb
                if g >= 2:
                    pb = g - 2
                    t.wait_ge(DV.h, dvt(pb // 4) + 3 + (pb % 4) + 1)
                reg = ctx.psA if (bb % 2 == 0) else ctx.psB
                for j in range(4):
                    dc = bb * 4 + j
                    ins = t.matmul(reg[:, j * 128:(j + 1) * 128],
                                   xnbf[:, dc * 128:(dc + 1) * 128],
                                   ctx.eye[:], start=True, stop=True)
                MM.inc(ins)

    @ctx.block.gpsimd
    def _(g):
        for c in range(NCH):
            g.wait_ge(DV.h, dvt(4 * c + 3) + 7)
            QG.issue(g, ctx.xnT_scr[c], stg)

    assert MM.n == mm0 + 4 * NTILE
    assert DV.n == dv0 + 7 * NTILE
    assert AC.n == ac0 + 3 * NTILE


def phase_stage(ctx, which):
    MM, DV, AC = ctx.MM, ctx.DV, ctx.AC
    QS, QG = ctx.QS, ctx.QG
    b = ctx.bf
    rope = which in ("q", "k")
    w_d = {"q": ctx.wq_d, "k": ctx.wk_d, "v": ctx.wv_d}[which]
    scr = {"q": ctx.q_scr, "k": ctx.k_scr, "v": ctx.v_scr}[which]
    bias_t = {"q": ctx.bq_t, "k": ctx.bk_t, "v": ctx.bv_t}[which]
    evac_scale = SCALE if which == "q" else 1.0

    cos4 = b(4096, 8192)
    sin4 = b(12288, 8192)
    wg2 = b(20480, 16384)
    xnt2 = b(36864, 16384)
    qg2 = b(53248, 4096)
    shq2 = b(57344, 4096)
    t1 = b(61440, 2048)
    t2 = b(63488, 2048)
    rot2 = b(65536, 4096)

    mm0, dv0, ac0 = MM.n, DV.n, AC.n
    qs0, qg0 = QS.g, QG.g
    NI = 32
    kmm = 2 if rope else 1
    kac = 2 if rope else 1

    def mm_proj(it):
        return mm0 + kmm * it + 1

    def mm_all(it):
        return mm0 + kmm * it + kmm

    def ac_evac(it):
        return ac0 + kac * it + 1

    def ac_all(it):
        return ac0 + kac * it + kac

    def dv_rot(it):
        return dv0 + it + 1

    def g_store(it, hl):
        return (qg0 + 2 + 4 * it + hl + (1 if it > 7 else 0)
                + (1 if it > 15 else 0))

    def g_wload(hg):
        return [qg0, qg0 + 1, qg0 + 34, qg0 + 67][hg]

    g_xnt = [qs0 + it for it in range(NI)]

    def _wload(g, hg):
        if hg >= 2:
            g.wait_ge(MM.h, mm_proj((hg - 2) * 8 + 7))
        wsrc = w_d[:, hg * 512:(hg + 1) * 512]
        wsrc = wsrc.rearrange("(dc p) n -> p dc n", p=128)
        wdst = wg2[:, (hg % 2) * 8192:(hg % 2) * 8192 + 8192]
        wdst = wdst.rearrange("p (dc n) -> p dc n", dc=16)
        assert QG.g == g_wload(hg), (QG.g, g_wload(hg))
        QG.issue(g, wdst, wsrc)

    @ctx.block.gpsimd
    def _(g):
        _wload(g, 0)
        _wload(g, 1)
        for it in range(NI):
            hg, tc = it // 8, it % 8
            seq, tl = tc // 4, (tc % 4) * 512
            if rope:
                g.wait_ge(DV.h, dv_rot(it))
                stsrc = rot2[:, (it % 2) * 2048:(it % 2) * 2048 + 2048]
            else:
                g.wait_ge(AC.h, ac_evac(it))
                stsrc = qg2[:, (it % 2) * 2048:(it % 2) * 2048 + 2048]
            for hl in range(4):
                u = seq * NH + hg * 4 + hl
                assert QG.g == g_store(it, hl)
                QG.issue(g, scr[u, :, tl:tl + 512],
                         stsrc[:, hl * 512:(hl + 1) * 512])
            if it == 7:
                _wload(g, 2)
            if it == 15:
                _wload(g, 3)

    @ctx.block.sync
    def _(sync):
        for it in range(NI):
            if it >= 2:
                sync.wait_ge(MM.h, mm_proj(it - 2))
            tc = it % 8
            QS.issue(sync, xnt2[:, (it % 2) * 8192:(it % 2) * 8192 + 8192],
                     ctx.xnT_scr[tc])

    @ctx.block.tensor
    def _(t):
        for it in range(NI):
            hg = it // 8
            QS.wait(t, g_xnt[it])
            QG.wait(t, g_wload(hg))
            if it >= 2:
                t.wait_ge(AC.h, ac_all(it - 2))
            reg = ctx.psA if (it % 2 == 0) else ctx.psB
            wg = wg2[:, (hg % 2) * 8192:(hg % 2) * 8192 + 8192]
            xn = xnt2[:, (it % 2) * 8192:(it % 2) * 8192 + 8192]
            for hl in range(4):
                for dc in range(NDC):
                    ins = t.matmul(
                        reg[:, hl * 512:(hl + 1) * 512],
                        wg[:, dc * 512 + hl * 128:dc * 512 + hl * 128 + 128],
                        xn[:, dc * 512:(dc + 1) * 512],
                        start=(dc == 0), stop=(dc == NDC - 1))
            MM.inc(ins)
            if rope:
                t.wait_ge(AC.h, ac_evac(it))
                qg = qg2[:, (it % 2) * 2048:(it % 2) * 2048 + 2048]
                for hl in range(4):
                    ins = t.matmul(reg[:, hl * 512:(hl + 1) * 512],
                                   ctx.mshuf[:], qg[:, hl * 512:(hl + 1) * 512],
                                   start=True, stop=True)
                MM.inc(ins)

    @ctx.block.scalar
    def _(s):
        for it in range(NI):
            hg = it // 8
            reg = ctx.psA if (it % 2 == 0) else ctx.psB
            qg = qg2[:, (it % 2) * 2048:(it % 2) * 2048 + 2048]
            s.wait_ge(MM.h, mm_proj(it))
            if it >= 2:
                if rope:
                    s.wait_ge(DV.h, dv_rot(it - 2))
                else:
                    QG.wait_many(s, [g_store(it - 2, hl) for hl in range(4)])
            for hl in range(4):
                h = hg * 4 + hl
                ins = s.activation(qg[:, hl * 512:(hl + 1) * 512],
                                   reg[:, hl * 512:(hl + 1) * 512],
                                   AF.Identity, scale=np.float32(evac_scale),
                                   bias=bias_t[:, h:h + 1])
            AC.inc(ins)
            if rope:
                s.wait_ge(MM.h, mm_all(it))
                shq = shq2[:, (it % 2) * 2048:(it % 2) * 2048 + 2048]
                if it >= 2:
                    s.wait_ge(DV.h, dv_rot(it - 2))
                ins = s.activation(shq, reg[:, 0:2048], AF.Identity)
                s.drain()
                AC.inc(ins)

    if rope:
        @ctx.block.vector
        def _(v):
            for it in range(NI):
                tc = it % 8
                v.wait_ge(AC.h, ac_all(it))
                if it >= 2:
                    QG.wait_many(v, [g_store(it - 2, hl) for hl in range(4)])
                qg = qg2[:, (it % 2) * 2048:(it % 2) * 2048 + 2048]
                shq = shq2[:, (it % 2) * 2048:(it % 2) * 2048 + 2048]
                coff = (tc % 4) * 2048
                v.tensor_mul(t1, qg, cos4[:, coff:coff + 2048])
                v.tensor_mul(t2, shq, sin4[:, coff:coff + 2048])
                v.drain()
                rot = rot2[:, (it % 2) * 2048:(it % 2) * 2048 + 2048]
                ins = v.tensor_add(rot, t1, t2)
                v.drain()
                DV.inc(ins)

    assert MM.n == mm0 + kmm * NI
    assert AC.n == ac0 + kac * NI
    if rope:
        assert DV.n == dv0 + NI


def phase_b(ctx):
    MM, DV, AC = ctx.MM, ctx.DV, ctx.AC
    QS = ctx.QS
    b = ctx.bf

    qkvb = b(20480, 12288)
    vsb = b(32768, 2048)
    psb2 = b(34816, 4096)
    ptsb2 = b(38912, 4096)
    otsb2 = b(43008, 4096)

    mm0, dv0, ac0 = MM.n, DV.n, AC.n
    qs0 = QS.g
    nb = [(qt // 4) + 1 for qt in range(16)]

    mm_vt, mm_s, mm_tb, mm_pv = {}, {}, {}, {}
    mm = mm0
    tb_seq = []
    tb_gidx = {}
    for u in range(NUNIT):
        mm += 1
        mm_vt[u] = mm
        for step in range(17):
            if step < 16:
                mm += 1
                mm_s[(u, step, 0)] = mm
                mm += 1
                mm_s[(u, step, 1)] = mm
            if step >= 1:
                qt = step - 1
                for bb in range(nb[qt]):
                    mm += 1
                    mm_tb[(u, qt, bb)] = mm
                    tb_gidx[(u, qt, bb)] = len(tb_seq)
                    tb_seq.append((u, qt, bb))
                    if bb >= 2:
                        mm += 1
                        mm_pv[(u, qt, bb - 2)] = mm
                for bb2 in range(max(0, nb[qt] - 2), nb[qt]):
                    mm += 1
                    mm_pv[(u, qt, bb2)] = mm
    mm_end = mm

    ac_e = {}
    ac = ac0
    for u in range(NUNIT):
        for qt in range(16):
            ac += 1
            ac_e[(u, qt, 0)] = ac
            ac += 1
            ac_e[(u, qt, 1)] = ac
    ac_end = ac

    dv_vt, dv_dD, dv_pte, dv_ot = {}, {}, {}, {}
    dv = dv0
    for u in range(NUNIT):
        dv += 4
        dv_vt[u] = dv
        for step in range(17):
            if step >= 1:
                qt = step - 1
                for bb in range(nb[qt]):
                    dv += 1
                    dv_pte[(u, qt, bb)] = dv
                dv += 1
                dv_ot[(u, qt)] = dv
            if step < 16:
                dv += 1
                dv += 1
                dv_dD[(u, step)] = dv
    dv_end = dv

    def g_load(u, j):
        return qs0 + 4 * u + j - (1 if u >= 1 else 0)

    def g_ostore(u):
        if u < NUNIT - 1:
            return qs0 + 4 * (u + 1) + 2
        return qs0 + 4 * (NUNIT - 1) + 3

    def ubufs(u):
        o = (u % 2) * 6144
        return (qkvb[:, o:o + 2048], qkvb[:, o + 2048:o + 4096],
                qkvb[:, o + 4096:o + 6144])

    def prev_same_qt(u, qt):
        if qt >= 2:
            return (u, qt - 2)
        if u >= 1:
            return (u - 1, 14 + qt)
        return None

    @ctx.block.sync
    def _(sync):
        for u in range(NUNIT):
            if u >= 2:
                sync.wait_ge(MM.h, mm_s[(u - 2, 15, 1)])
            qb, kb, vb = ubufs(u)
            assert QS.g == g_load(u, 0)
            QS.issue(sync, qb, ctx.q_scr[u])
            QS.issue(sync, kb, ctx.k_scr[u])
            QS.issue(sync, vb, ctx.v_scr[u])
            if u >= 1:
                sync.wait_ge(DV.h, dv_ot[(u - 1, 15)])
                assert QS.g == g_ostore(u - 1)
                QS.issue(sync, ctx.oT_scr[u - 1],
                         otsb2[:, ((u - 1) % 2) * 2048:
                               ((u - 1) % 2) * 2048 + 2048])
        sync.wait_ge(DV.h, dv_ot[(NUNIT - 1, 15)])
        assert QS.g == g_ostore(NUNIT - 1)
        QS.issue(sync, ctx.oT_scr[NUNIT - 1],
                 otsb2[:, ((NUNIT - 1) % 2) * 2048:
                       ((NUNIT - 1) % 2) * 2048 + 2048])

    @ctx.block.tensor
    def _(t):
        for u in range(NUNIT):
            qb, kb, vb = ubufs(u)
            QS.wait_many(t, [g_load(u, j) for j in range(3)])
            if u >= 1:
                t.wait_ge(AC.h, ac_e[(u - 1, 15, 1)])
            for kt in range(16):
                ins = t.matmul(ctx.psA[:, kt * 128:(kt + 1) * 128],
                               vb[:, kt * 128:(kt + 1) * 128], ctx.eye[:],
                               start=True, stop=True)
            MM.inc(ins)
            for step in range(17):
                if step < 16:
                    qt = step
                    lhs = qb[:, qt * 128:(qt + 1) * 128]
                    for h in range(2):
                        if qt == 0:
                            t.wait_ge(DV.h, dv_vt[u])
                        else:
                            t.wait_ge(AC.h, ac_e[(u, qt - 1, h)])
                        for kk in range(2 * h, 2 * h + 2):
                            ins = t.matmul(ctx.psA[:, kk * 512:(kk + 1) * 512],
                                           lhs, kb[:, kk * 512:(kk + 1) * 512],
                                           start=True, stop=True)
                        MM.inc(ins)
                if step >= 1:
                    qt = step - 1
                    t.wait_ge(DV.h, dv_dD[(u, qt)])
                    psb = psb2[:, (qt % 2) * 2048:(qt % 2) * 2048 + 2048]
                    dq = ctx.dq2[:, (qt % 2) * 128:(qt % 2) * 128 + 128]
                    pts = ptsb2[:, (qt % 2) * 2048:(qt % 2) * 2048 + 2048]

                    def pv_partial(bb2, qt=qt, u=u, pts=pts, t=t):
                        t.wait_ge(DV.h, dv_pte[(u, qt, bb2)])
                        if bb2 == 0 and (u, qt) != (0, 0):
                            pu, pq = (u, qt - 1) if qt > 0 else (u - 1, 15)
                            t.wait_ge(DV.h, dv_ot[(pu, pq)])
                        lo = bb2 * 4
                        hi = min(qt, lo + 3)
                        for kt in range(lo, hi + 1):
                            ins = t.matmul(ctx.psB[:, 1024:1152],
                                           vsb[:, kt * 128:(kt + 1) * 128],
                                           pts[:, kt * 128:(kt + 1) * 128],
                                           start=(kt == 0), stop=(kt == qt))
                        MM.inc(ins)

                    for bb in range(nb[qt]):
                        g = tb_gidx[(u, qt, bb)]
                        if g >= 3:
                            t.wait_ge(DV.h, dv_pte[tb_seq[g - 3]])
                        reg = (ctx.psB[:, 0:512], ctx.psB[:, 512:1024],
                               ctx.psB[:, 1536:2048])[g % 3]
                        for j in range(4):
                            kt = bb * 4 + j
                            if kt > qt:
                                break
                            ins = t.matmul(reg[:, j * 128:(j + 1) * 128],
                                           psb[:, kt * 128:(kt + 1) * 128],
                                           dq, start=True, stop=True)
                        MM.inc(ins)
                        if bb >= 2:
                            pv_partial(bb - 2)
                    for bb2 in range(max(0, nb[qt] - 2), nb[qt]):
                        pv_partial(bb2)

    @ctx.block.scalar
    def _(s):
        for u in range(NUNIT):
            for qt in range(16):
                psb = psb2[:, (qt % 2) * 2048:(qt % 2) * 2048 + 2048]
                pv = prev_same_qt(u, qt)
                for h in range(2):
                    acc = (ctx.accA2 if h == 0 else ctx.accB2)[:, (qt % 2):(qt % 2) + 1]
                    if pv is not None:
                        if h == 0:
                            s.wait_ge(MM.h, mm_tb[(pv[0], pv[1], nb[pv[1]] - 1)])
                        s.wait_ge(DV.h, dv_dD[pv])
                    s.wait_ge(MM.h, mm_s[(u, qt, h)])
                    ins = s.activation(psb[:, h * 1024:h * 1024 + 1024],
                                       ctx.psA[:, h * 1024:h * 1024 + 1024],
                                       AF.Exp, bias=np.float32(-SHIFT),
                                       accum_out=acc)
                    AC.inc(ins)

    @ctx.block.vector
    def _(v):
        for u in range(NUNIT):
            for bb in range(4):
                v.wait_ge(MM.h, mm_vt[u])
                ins = v.tensor_copy(vsb[:, bb * 512:(bb + 1) * 512],
                                    ctx.psA[:, bb * 512:(bb + 1) * 512])
                DV.inc(ins)
            for step in range(17):
                if step >= 1:
                    qt = step - 1
                    pts = ptsb2[:, (qt % 2) * 2048:(qt % 2) * 2048 + 2048]
                    for bb in range(nb[qt]):
                        v.wait_ge(MM.h, mm_tb[(u, qt, bb)])
                        nblk = min(4, qt + 1 - bb * 4)
                        g = tb_gidx[(u, qt, bb)]
                        reg = (ctx.psB[:, 0:512], ctx.psB[:, 512:1024],
                               ctx.psB[:, 1536:2048])[g % 3]
                        ins = v.tensor_copy(
                            pts[:, bb * 512:bb * 512 + nblk * 128],
                            reg[:, 0:nblk * 128])
                        DV.inc(ins)
                    v.wait_ge(MM.h, mm_pv[(u, qt, nb[qt] - 1)])
                    if qt == 0 and u >= 2:
                        QS.wait(v, g_ostore(u - 2))
                    ot = otsb2[:, (u % 2) * 2048:(u % 2) * 2048 + 2048]
                    ins = v.tensor_copy(ot[:, qt * 128:(qt + 1) * 128],
                                        ctx.psB[:, 1024:1152])
                    DV.inc(ins)
                if step < 16:
                    qt = step
                    psb = psb2[:, (qt % 2) * 2048:(qt % 2) * 2048 + 2048]
                    dq = ctx.dq2[:, (qt % 2) * 128:(qt % 2) * 128 + 128]
                    accA = ctx.accA2[:, (qt % 2):(qt % 2) + 1]
                    accB = ctx.accB2[:, (qt % 2):(qt % 2) + 1]
                    v.wait_ge(AC.h, ac_e[(u, qt, 1)])
                    pv = prev_same_qt(u, qt)
                    if pv is not None:
                        v.wait_ge(MM.h, mm_tb[(pv[0], pv[1], nb[pv[1]] - 1)])
                    ins = v.tensor_mul(psb[:, qt * 128:(qt + 1) * 128],
                                       psb[:, qt * 128:(qt + 1) * 128],
                                       ctx.tril[:])
                    DV.inc(ins)
                    v.tensor_add(ctx.dsum[:], accA, accB)
                    v.drain()
                    v.reciprocal(ctx.dinv[:], ctx.dsum[:])
                    v.drain()
                    ins = v.tensor_scalar(dq, ctx.eye[:], ctx.dinv[:, 0:1],
                                          None, ALU.mult)
                    v.drain()
                    DV.inc(ins)

    assert MM.n == mm_end
    assert AC.n == ac_end
    assert DV.n == dv_end


def phase_c(ctx):
    MM, DV, AC = ctx.MM, ctx.DV, ctx.AC
    QS, QG, QC = ctx.QS, ctx.QG, ctx.QC
    b, f = ctx.bf, ctx.f32

    wo = b(0, 32768)
    ot2 = b(32768, 16384)
    xn2bf = b(49152, 2048)
    stg = b(51200, 8192)
    xt2 = f(0, 2 * D)
    ysb = f(2 * D, D)
    junk = f(3 * D, D)
    xpo2 = f(4 * D, 2 * D)

    mm0, dv0, ac0 = MM.n, DV.n, AC.n
    qs0, qg0, qc0 = QS.g, QG.g, QC.g

    mm_proj, mm_tb = {}, {}
    mm = mm0
    for i in range(NTILE + 1):
        if i < NTILE:
            mm += 1
            mm_proj[i] = mm
        if i >= 1:
            for bb in range(4):
                mm += 1
                mm_tb[(i - 1, bb)] = mm
    dv_ev, dv_l = {}, {}
    dv = dv0
    for i in range(NTILE + 1):
        if i >= 1:
            for bb in range(4):
                dv += 1
                dv_ev[(i - 1, bb)] = dv
        if i < NTILE:
            for j in range(7):
                dv += 1
                dv_l[(i, j + 1)] = dv
    def act(i):
        return ac0 + 5 * i

    def g_ot(c, h):
        return qs0 + 20 * c + h

    def g_x(i):
        return qs0 + 20 * (i // 4) + 16 + (i % 4)

    def g_xp(i):
        return qc0 + i + (i // 4)

    def g_x2t(c):
        return qc0 + 5 * c + 4

    @ctx.block.gpsimd
    def _(g):
        wsrc = ctx.wo_d[:, :].rearrange("(dc p) n -> p dc n", p=128)
        wdst = wo.rearrange("p (dc n) -> p dc n", dc=16)
        QG.issue(g, wdst, wsrc)

    g_wo = qg0

    @ctx.block.sync
    def _(sync):
        for c in range(NCH):
            seq, tl = c // 4, (c % 4) * 512
            if c >= 2:
                sync.wait_ge(MM.h, mm_proj[4 * (c - 2) + 3])
            for h in range(NH):
                u = seq * NH + h
                QS.issue(sync, ot2[:, (c % 2) * 8192 + h * 512:
                                   (c % 2) * 8192 + (h + 1) * 512],
                         ctx.oT_scr[u, :, tl:tl + 512])
            for j in range(4):
                i = 4 * c + j
                if i >= 2:
                    sync.wait_ge(DV.h, dv_l[(i - 2, 4)])
                assert QS.g == g_x(i)
                QS.issue(sync, xt2[:, (i % 2) * D:(i % 2) * D + D],
                         ctx.x_d[i * 128:(i + 1) * 128, :])

    @ctx.block.tensor
    def _(t):
        for i in range(NTILE + 1):
            if i < NTILE:
                c = i // 4
                QS.wait_many(t, [g_ot(c, h) for h in range(NH)])
                QG.wait(t, g_wo)
                if i >= 1:
                    t.wait_ge(DV.h, dv_l[(i - 1, 1)])
                ot = ot2[:, (c % 2) * 8192:(c % 2) * 8192 + 8192]
                j = i % 4
                for dc in range(NDC):
                    lhs = ot[:, dc * 512 + j * 128:dc * 512 + j * 128 + 128]
                    for nbk in range(4):
                        ins = t.matmul(ctx.psA[:, nbk * 512:(nbk + 1) * 512],
                                       lhs,
                                       wo[:, dc * 2048 + nbk * 512:
                                          dc * 2048 + (nbk + 1) * 512],
                                       start=(dc == 0), stop=(dc == NDC - 1))
                MM.inc(ins)
            if i >= 1:
                ti = i - 1
                t.wait_ge(AC.h, act(ti) + 5)
                for bb in range(4):
                    g = 4 * ti + bb
                    if g >= 2:
                        pb = g - 2
                        t.wait_ge(DV.h, dv_ev[(pb // 4, pb % 4)])
                    reg = (ctx.psB[:, 0:512] if bb % 2 == 0
                           else ctx.psB[:, 512:1024])
                    for j2 in range(4):
                        dc = bb * 4 + j2
                        ins = t.matmul(reg[:, j2 * 128:(j2 + 1) * 128],
                                       xn2bf[:, dc * 128:(dc + 1) * 128],
                                       ctx.eye[:], start=True, stop=True)
                    MM.inc(ins)

    @ctx.block.vector
    def _(v):
        for i in range(NTILE + 1):
            if i >= 1:
                ti = i - 1
                j = ti % 4
                for bb in range(4):
                    v.wait_ge(MM.h, mm_tb[(ti, bb)])
                    if j == 0 and bb == 0 and ti >= 4:
                        QC.wait(v, g_x2t(ti // 4 - 1))
                    reg = (ctx.psB[:, 0:512] if bb % 2 == 0
                           else ctx.psB[:, 512:1024])
                    dst = stg.rearrange("p (dc c) -> p dc c", dc=16)
                    dst = dst[:, bb * 4:(bb + 1) * 4, j * 128:(j + 1) * 128]
                    srcv = reg[:, 0:512].rearrange("p (dc c) -> p dc c", dc=4)
                    ins = v.tensor_copy(dst, srcv)
                    DV.inc(ins)
            if i < NTILE:
                xt = xt2[:, (i % 2) * D:(i % 2) * D + D]
                xpo = xpo2[:, (i % 2) * D:(i % 2) * D + D]
                v.wait_ge(MM.h, mm_proj[i])
                ins = v.scalar_tensor_tensor(ysb, ctx.psA[:, 0:2048], 1.0,
                                             ctx.bo_bc[:], ALU.mult, ALU.add,
                                             accum_out=ctx.sx[:])
                v.drain()
                DV.inc(ins)
                v.wait_ge(AC.h, act(i) + 1)
                v.tensor_scalar(ctx.m1[:], ctx.sx[:], 1.0 / D, None, ALU.mult)
                v.drain()
                v.tensor_mul(ctx.m2[:], ctx.m1[:], ctx.m1[:])
                v.drain()
                ins = v.scalar_tensor_tensor(ctx.var[:], ctx.sxx[:], 1.0 / D,
                                             ctx.m2[:], ALU.mult, ALU.subtract)
                v.drain()
                DV.inc(ins)
                v.wait_ge(AC.h, act(i) + 2)
                ins = v.scalar_tensor_tensor(ctx.nmr[:], ctx.m1[:], -1.0,
                                             ctx.rstd[:], ALU.mult, ALU.mult)
                v.drain()
                DV.inc(ins)
                QS.wait(v, g_x(i))
                ins = v.scalar_tensor_tensor(ysb, ysb, ctx.rstd[:, 0:1], xt,
                                             ALU.mult, ALU.add)
                v.drain()
                DV.inc(ins)
                if i >= 2:
                    QC.wait(v, g_xp(i - 2))
                    v.wait_ge(AC.h, act(i - 2) + 5)
                ins = v.tensor_scalar(xpo, ysb, ctx.nmr[:, 0:1], None, ALU.add,
                                      ALU.add, accum_out=ctx.sx3[:])
                v.drain()
                DV.inc(ins)
                v.wait_ge(AC.h, act(i) + 3)
                v.tensor_scalar(ctx.m13[:], ctx.sx3[:], 1.0 / D, None, ALU.mult)
                v.drain()
                v.tensor_mul(ctx.m2[:], ctx.m13[:], ctx.m13[:])
                v.drain()
                ins = v.scalar_tensor_tensor(ctx.var[:], ctx.sxx3[:], 1.0 / D,
                                             ctx.m2[:], ALU.mult, ALU.subtract)
                v.drain()
                DV.inc(ins)
                v.wait_ge(AC.h, act(i) + 4)
                ins = v.scalar_tensor_tensor(ctx.nmr[:], ctx.m13[:], -1.0,
                                             ctx.rstd[:], ALU.mult, ALU.mult)
                v.drain()
                DV.inc(ins)

    @ctx.block.scalar
    def _(s):
        for i in range(NTILE):
            xpo = xpo2[:, (i % 2) * D:(i % 2) * D + D]
            s.wait_ge(DV.h, dv_l[(i, 1)])
            if i >= 1:
                s.wait_ge(DV.h, dv_l[(i - 1, 2)])
            ins = s.activation(junk, ysb, AF.Square, accum_out=ctx.sxx[:])
            AC.inc(ins)
            s.wait_ge(DV.h, dv_l[(i, 2)])
            s.activation(ctx.tln[:], ctx.var[:], AF.Ln, bias=np.float32(LNEPS))
            s.drain()
            ins = s.activation(ctx.rstd[:], ctx.tln[:], AF.Exp,
                               scale=np.float32(-0.5))
            s.drain()
            AC.inc(ins)
            s.wait_ge(DV.h, dv_l[(i, 5)])
            if i >= 1:
                s.wait_ge(DV.h, dv_l[(i - 1, 6)])
            ins = s.activation(junk, xpo, AF.Square, accum_out=ctx.sxx3[:])
            AC.inc(ins)
            s.wait_ge(DV.h, dv_l[(i, 6)])
            s.activation(ctx.tln[:], ctx.var[:], AF.Ln, bias=np.float32(LNEPS))
            s.drain()
            ins = s.activation(ctx.rstd[:], ctx.tln[:], AF.Exp,
                               scale=np.float32(-0.5))
            s.drain()
            AC.inc(ins)
            s.wait_ge(DV.h, dv_l[(i, 7)])
            if i >= 1:
                s.wait_ge(MM.h, mm_tb[(i - 1, 3)])
            ins = s.activation(xn2bf, xpo, AF.Identity,
                               scale=ctx.rstd[:, 0:1], bias=ctx.nmr[:, 0:1])
            s.drain()
            AC.inc(ins)
            s.wait_ge(DV.h, dv_l[(i, 5)])
            assert QC.g == g_xp(i)
            QC.issue(s, ctx.xp_scr[i], xpo)
            if i % 4 == 3:
                s.wait_ge(DV.h, dv_ev[(i, 3)])
                assert QC.g == g_x2t(i // 4)
                QC.issue(s, ctx.xn2T_scr[i // 4], stg)

    assert MM.n == mm0 + 5 * NTILE
    assert DV.n == dv0 + 11 * NTILE
    assert AC.n == ac0 + 5 * NTILE


def phase_d(ctx):
    MM, DV, AC = ctx.MM, ctx.DV, ctx.AC
    QS, QC = ctx.QS, ctx.QC
    b, f = ctx.bf, ctx.f32

    xn2t2 = b(0, 16384)
    ht = b(16384, 22528)
    ytsb = b(38912, 8192)
    q12 = b(47104, 16384)
    q3b = b(63488, 4096)
    s1b = b(67584, 1024)
    xp2 = f(0, 2 * D)
    out2 = f(2 * D, 2 * D)
    ysb2 = f(4 * D, 2 * D)

    mm0, dv0, ac0 = MM.n, DV.n, AC.n
    qs0, qc0 = QS.g, QC.g

    def mm_x1(c, hc):
        return mm0 + 180 * c + 2 * hc + 1

    def mm_x2(c, hc):
        return mm0 + 180 * c + 2 * hc + 2

    def mm_y(c, half, hc):
        return mm0 + 180 * c + 88 + 44 * half + hc + 1

    def mm_tr(c, tb):
        return mm0 + 180 * c + 176 + tb + 1

    def ac_sig(c, hc):
        return ac0 + 60 * c + hc + 1

    def ac_ye(c, half, k):
        return ac0 + 60 * c + 44 + 4 * half + k + 1

    def ac_sq(c, tb):
        return ac0 + 60 * c + 52 + 2 * tb + 1

    def ac_rstd(c, tb):
        return ac0 + 60 * c + 52 + 2 * tb + 2

    def dv_ht(c, hc):
        return dv0 + 72 * c + hc + 1

    def dv_ye(c, half, k):
        return dv0 + 72 * c + 44 + 4 * half + k + 1

    _EV = {0: 1, 1: 2, 2: 7, 3: 12}
    _CH = {0: 3, 1: 8, 2: 13, 3: 17}

    def dv_ev(c, tb):
        return dv0 + 72 * c + 52 + _EV[tb]

    def dv_ch(c, tb, j):
        return dv0 + 72 * c + 52 + _CH[tb] + j

    def g_xn(c):
        return qs0 + 181 * c

    def g_q1(c, hc):
        return qs0 + 181 * c + 2 * hc + 1

    def g_q2(c, hc):
        return qs0 + 181 * c + 2 * hc + 2

    def g_q3(c, half, hc):
        return qs0 + 181 * c + 89 + 44 * half + hc

    def g_xpl(c, tb):
        return qs0 + 181 * c + 177 + tb

    def g_out(c, tb):
        return qc0 + 4 * c + tb

    @ctx.block.sync
    def _(sync):
        for c in range(NCH):
            if c >= 2:
                sync.wait_ge(MM.h, mm_x2(c - 2, NHC - 1))
            assert QS.g == g_xn(c)
            QS.issue(sync, xn2t2[:, (c % 2) * 8192:(c % 2) * 8192 + 8192],
                     ctx.xn2T_scr[c])
            for hc in range(NHC):
                k = c * NHC + hc
                if k >= 4:
                    pk = k - 4
                    pc, ph = pk // NHC, pk % NHC
                    sync.wait_ge(MM.h, mm_x1(pc, ph))
                QS.issue(sync, q12[:, (k % 4) * 2048:(k % 4) * 2048 + 2048],
                         ctx.q1T_scr[hc])
                if k >= 4:
                    pk = k - 4
                    pc, ph = pk // NHC, pk % NHC
                    sync.wait_ge(MM.h, mm_x2(pc, ph))
                QS.issue(sync, q12[:, 8192 + (k % 4) * 2048:
                                   8192 + (k % 4) * 2048 + 2048],
                         ctx.q2T_scr[hc])
            for half in range(2):
                for hc in range(NHC):
                    k = c * 2 * NHC + half * NHC + hc
                    if k >= 4:
                        kk = k - 4
                        pc = kk // (2 * NHC)
                        ph, phc = (kk % (2 * NHC)) // NHC, kk % NHC
                        sync.wait_ge(MM.h, mm_y(pc, ph, phc))
                    wsrc = ctx.q3n_scr[half * 1024:(half + 1) * 1024,
                                       hc * 128:(hc + 1) * 128]
                    assert QS.g == g_q3(c, half, hc)
                    QS.issue(sync, q3b[:, (k % 4) * 1024:(k % 4) * 1024 + 1024],
                             wsrc, transpose=True)
            for tb in range(4):
                g = 4 * c + tb
                if g >= 2:
                    sync.wait_ge(DV.h, dv_ch((g - 2) // 4, (g - 2) % 4, 3))
                assert QS.g == g_xpl(c, tb)
                QS.issue(sync, xp2[:, (tb % 2) * D:(tb % 2) * D + D],
                         ctx.xp_scr[g])

    @ctx.block.tensor
    def _(t):
        for c in range(NCH):
            xn = xn2t2[:, (c % 2) * 8192:(c % 2) * 8192 + 8192]
            QS.wait(t, g_xn(c))
            for hc in range(NHC):
                k = c * NHC + hc
                QS.wait(t, g_q1(c, hc))
                if c >= 1 and hc == 0:
                    t.wait_ge(DV.h, dv_ev(c - 1, 2))
                if k >= 2:
                    pc, ph = (c, hc - 2) if hc >= 2 else (c - 1, NHC - 2 + hc)
                    t.wait_ge(AC.h, ac_sig(pc, ph))
                q1 = q12[:, (k % 4) * 2048:(k % 4) * 2048 + 2048]
                reg1 = ctx.psA[:, (hc % 2) * 512:(hc % 2) * 512 + 512]
                for dc in range(NDC):
                    ins = t.matmul(reg1, q1[:, dc * 128:(dc + 1) * 128],
                                   xn[:, dc * 512:(dc + 1) * 512],
                                   start=(dc == 0), stop=(dc == NDC - 1))
                MM.inc(ins)
                QS.wait(t, g_q2(c, hc))
                if k >= 2:
                    pc, ph = (c, hc - 2) if hc >= 2 else (c - 1, NHC - 2 + hc)
                    t.wait_ge(DV.h, dv_ht(pc, ph))
                q2 = q12[:, 8192 + (k % 4) * 2048:8192 + (k % 4) * 2048 + 2048]
                reg2 = ctx.psA[:, 1024 + (hc % 2) * 512:
                               1024 + (hc % 2) * 512 + 512]
                for dc in range(NDC):
                    ins = t.matmul(reg2, q2[:, dc * 128:(dc + 1) * 128],
                                   xn[:, dc * 512:(dc + 1) * 512],
                                   start=(dc == 0), stop=(dc == NDC - 1))
                MM.inc(ins)
            for half in range(2):
                if half == 0:
                    t.wait_ge(AC.h, ac_sig(c, NHC - 1))
                    t.wait_ge(DV.h, dv_ht(c, NHC - 1))
                    if c >= 1:
                        t.wait_ge(DV.h, dv_ev(c - 1, 3))
                for hc in range(NHC):
                    k = half * NHC + hc
                    QS.wait(t, g_q3(c, half, hc))
                    kg = (c * 2 * NHC + k) % 4
                    q3 = q3b[:, kg * 1024:kg * 1024 + 1024]
                    for dcl in range(8):
                        if half == 1 and hc == 0:
                            if dcl < 4:
                                t.wait_ge(DV.h, dv_ye(c, 0, dcl))
                            else:
                                t.wait_ge(AC.h, ac_ye(c, 0, dcl - 4))
                        reg = ctx.psA if dcl < 4 else ctx.psB
                        ins = t.matmul(
                            reg[:, (dcl % 4) * 512:(dcl % 4) * 512 + 512],
                            q3[:, dcl * 128:(dcl + 1) * 128],
                            ht[:, hc * 512:(hc + 1) * 512],
                            start=(hc == 0), stop=(hc == NHC - 1))
                    MM.inc(ins)
            for tb in range(4):
                if tb == 0:
                    t.wait_ge(DV.h, dv_ye(c, 1, 3))
                    t.wait_ge(AC.h, ac_ye(c, 1, 3))
                elif tb >= 2:
                    t.wait_ge(DV.h, dv_ev(c, tb - 2))
                reg = ctx.psA if tb % 2 == 0 else ctx.psB
                for dc in range(NDC):
                    lhs = ytsb[:, dc * 512 + tb * 128:dc * 512 + tb * 128 + 128]
                    ins = t.matmul(reg[:, dc * 128:(dc + 1) * 128], lhs,
                                   ctx.eye[:], start=True, stop=True)
                MM.inc(ins)

    @ctx.block.scalar
    def _(s):
        for c in range(NCH):
            for hc in range(NHC):
                k = c * NHC + hc
                s.wait_ge(MM.h, mm_x1(c, hc))
                if k >= 2:
                    pc, ph = (c, hc - 2) if hc >= 2 else (c - 1, NHC - 2 + hc)
                    s.wait_ge(DV.h, dv_ht(pc, ph))
                reg1 = ctx.psA[:, (hc % 2) * 512:(hc % 2) * 512 + 512]
                ins = s.activation(s1b[:, (hc % 2) * 512:(hc % 2) * 512 + 512],
                                   reg1, AF.Sigmoid)
                AC.inc(ins)
            for half in range(2):
                s.wait_ge(MM.h, mm_y(c, half, NHC - 1))
                for k in range(4):
                    dc = half * 8 + 4 + k
                    ins = s.copy(ytsb[:, dc * 512:dc * 512 + 512],
                                 ctx.psB[:, k * 512:k * 512 + 512])
                    AC.inc(ins)
            for tb in range(4):
                g = 4 * c + tb
                ysb = ysb2[:, (tb % 2) * D:(tb % 2) * D + D]
                out = out2[:, (tb % 2) * D:(tb % 2) * D + D]
                if g >= 2:
                    QC.wait(s, g_out((g - 2) // 4, (g - 2) % 4))
                s.wait_ge(DV.h, dv_ev(c, tb))
                if (c, tb) != (0, 0):
                    pc, pt = (c, tb - 1) if tb > 0 else (c - 1, 3)
                    s.wait_ge(DV.h, dv_ch(pc, pt, 0))
                ins = s.activation(out, ysb, AF.Square, accum_out=ctx.sxx[:])
                AC.inc(ins)
                s.wait_ge(DV.h, dv_ch(c, tb, 0))
                s.activation(ctx.tln[:], ctx.var[:], AF.Ln, bias=np.float32(LNEPS))
                s.drain()
                ins = s.activation(ctx.rstd[:], ctx.tln[:], AF.Exp,
                               scale=np.float32(-0.5))
                s.drain()
                AC.inc(ins)
                s.wait_ge(DV.h, dv_ch(c, tb, 3))
                assert QC.g == g_out(c, tb)
                QC.issue(s, ctx.out_d[(c * 4 + tb) * 128:
                                      (c * 4 + tb) * 128 + 128, :], out)

    @ctx.block.vector
    def _(v):
        for c in range(NCH):
            for hc in range(NHC):
                v.wait_ge(MM.h, mm_x2(c, hc))
                v.wait_ge(AC.h, ac_sig(c, hc))
                if c >= 1 and hc == 0:
                    v.wait_ge(MM.h, mm_y(c - 1, 1, NHC - 1))
                reg2 = ctx.psA[:, 1024 + (hc % 2) * 512:
                               1024 + (hc % 2) * 512 + 512]
                ins = v.tensor_tensor(ht[:, hc * 512:(hc + 1) * 512], reg2,
                                      s1b[:, (hc % 2) * 512:(hc % 2) * 512 + 512],
                                      ALU.mult)
                DV.inc(ins)
            for half in range(2):
                v.wait_ge(MM.h, mm_y(c, half, NHC - 1))
                for k in range(4):
                    dc = half * 8 + k
                    ins = v.tensor_copy(
                        ytsb[:, dc * 512:(dc + 1) * 512],
                        ctx.psA[:, k * 512:k * 512 + 512])
                    DV.inc(ins)

            def ev(tb, c=c, v=v):
                v.wait_ge(MM.h, mm_tr(c, tb))
                if tb >= 2:
                    v.wait_ge(AC.h, ac_sq(c, tb - 2))
                reg = ctx.psA if tb % 2 == 0 else ctx.psB
                ysb = ysb2[:, (tb % 2) * D:(tb % 2) * D + D]
                ins = v.tensor_scalar(ysb, reg[:, 0:2048], 1.0, None,
                                      ALU.mult, ALU.add,
                                      accum_out=ctx.sx2[:, tb % 2:tb % 2 + 1])
                v.drain()
                DV.inc(ins)

            def ch(tb, c=c, v=v):
                v.wait_ge(AC.h, ac_sq(c, tb))
                v.tensor_scalar(ctx.m1[:], ctx.sx2[:, tb % 2:tb % 2 + 1],
                                1.0 / D, None, ALU.mult)
                v.drain()
                v.tensor_mul(ctx.m2[:], ctx.m1[:], ctx.m1[:])
                v.drain()
                ins = v.scalar_tensor_tensor(ctx.var[:], ctx.sxx[:], 1.0 / D,
                                             ctx.m2[:], ALU.mult, ALU.subtract)
                v.drain()
                DV.inc(ins)
                v.wait_ge(AC.h, ac_rstd(c, tb))
                ins = v.scalar_tensor_tensor(ctx.nmr[:], ctx.m1[:], -1.0,
                                             ctx.rstd[:], ALU.mult, ALU.mult)
                v.drain()
                DV.inc(ins)
                QS.wait(v, g_xpl(c, tb))
                g = 4 * c + tb
                if g >= 2:
                    QC.wait(v, g_out((g - 2) // 4, (g - 2) % 4))
                ysb = ysb2[:, (tb % 2) * D:(tb % 2) * D + D]
                out = out2[:, (tb % 2) * D:(tb % 2) * D + D]
                ins = v.scalar_tensor_tensor(out, ysb, ctx.rstd[:, 0:1],
                                             xp2[:, (tb % 2) * D:(tb % 2) * D + D],
                                             ALU.mult, ALU.add)
                v.drain()
                DV.inc(ins)
                ins = v.tensor_scalar(out, out, ctx.nmr[:, 0:1], None, ALU.add)
                v.drain()
                DV.inc(ins)

            ev(0); ev(1); ch(0); ev(2); ch(1); ev(3); ch(2); ch(3)

    @ctx.block.sync
    def _(sync):
        QC.wait_all(sync)

    assert MM.n == mm0 + 180 * NCH
    assert AC.n == ac0 + 60 * NCH
    assert DV.n == dv0 + 72 * NCH


def build():
    nc = bass.Bass("TRN2", target_bir_lowering=False)
    ctx = Ctx()
    ctx.nc = nc

    ctx.x_d = nc.dram_tensor("x", [NT, D], F32, kind="ExternalInput").ap()
    ctx.wq_d = nc.dram_tensor("wq", [D, D], F32, kind="ExternalInput").ap()
    ctx.wk_d = nc.dram_tensor("wk", [D, D], F32, kind="ExternalInput").ap()
    ctx.wv_d = nc.dram_tensor("wv", [D, D], F32, kind="ExternalInput").ap()
    ctx.wo_d = nc.dram_tensor("wo", [D, D], F32, kind="ExternalInput").ap()
    ctx.bq_d = nc.dram_tensor("bq", [1, D], F32, kind="ExternalInput").ap()
    ctx.bk_d = nc.dram_tensor("bk", [1, D], F32, kind="ExternalInput").ap()
    ctx.bv_d = nc.dram_tensor("bv", [1, D], F32, kind="ExternalInput").ap()
    ctx.bo_d = nc.dram_tensor("bo", [1, D], F32, kind="ExternalInput").ap()
    ctx.w1_d = nc.dram_tensor("w1", [HF, D], F32, kind="ExternalInput").ap()
    ctx.w2_d = nc.dram_tensor("w2", [HF, D], F32, kind="ExternalInput").ap()
    ctx.w3_d = nc.dram_tensor("w3", [D, HF], F32, kind="ExternalInput").ap()
    ctx.out_d = nc.dram_tensor("out", [NT, D], F32, kind="ExternalOutput").ap()

    ctx.xnT_scr = nc.dram_tensor("xnT_scr", [NCH, 128, 8192], BF16).ap()
    ctx.q_scr = nc.dram_tensor("q_scr", [NUNIT, 128, T], BF16).ap()
    ctx.k_scr = nc.dram_tensor("k_scr", [NUNIT, 128, T], BF16).ap()
    ctx.v_scr = nc.dram_tensor("v_scr", [NUNIT, 128, T], BF16).ap()
    ctx.oT_scr = nc.dram_tensor("oT_scr", [NUNIT, 128, T], BF16).ap()
    ctx.xp_scr = nc.dram_tensor("xp_scr", [NTILE, 128, D], F32).ap()
    ctx.xn2T_scr = nc.dram_tensor("xn2T_scr", [NCH, 128, 8192], BF16).ap()
    ctx.q1T_scr = nc.dram_tensor("q1T_scr", [NHC, 128, 2048], BF16).ap()
    ctx.q2T_scr = nc.dram_tensor("q2T_scr", [NHC, 128, 2048], BF16).ap()
    ctx.q3n_scr = nc.dram_tensor("q3n_scr", [D, HF], BF16).ap()

    with contextlib.ExitStack() as stack:
        ctx.ARB = stack.enter_context(
            nc.sbuf_tensor("arena_bf", [128, 69632], BF16))
        ctx.ARF = stack.enter_context(
            nc.sbuf_tensor("arena_f32", [128, 12544], F32))

        def bf(off, n):
            return ctx.ARB[:, off:off + n]

        def f32(off, n):
            return ctx.ARF[:, off:off + n]

        ctx.bf = bf
        ctx.f32 = f32

        def sb(name, shape, dt):
            t = stack.enter_context(nc.sbuf_tensor(name, shape, dt))
            setattr(ctx, name, t)

        sb("eye", [128, 128], BF16)
        sb("tril", [128, 128], BF16)
        sb("mshuf", [128, 128], BF16)
        sb("supe", [128, 128], BF16)
        sb("subo", [128, 128], BF16)
        sb("onesb", [128, 128], BF16)
        sb("erow", [1, 128], F32)
        sb("invf", [1, 128], F32)
        sb("onesrow", [1, 128], F32)
        sb("onescol", [128, 1], F32)
        sb("one11", [1, 1], F32)
        sb("bq_t", [128, 16], F32)
        sb("bk_t", [128, 16], F32)
        sb("bv_t", [128, 16], F32)
        sb("bo_bc", [128, D], F32)
        sb("gcols", [128, 48], F32)
        sb("gtot", [128, 1], F32)
        sb("thr11", [1, 1], F32)
        sb("thrbc", [128, 1], F32)
        sb("nthrbc", [128, 1], F32)
        sb("sx", [128, 1], F32)
        sb("sx2", [128, 2], F32)
        sb("sxx", [128, 1], F32)
        sb("sx3", [128, 1], F32)
        sb("sxx3", [128, 1], F32)
        sb("m1", [128, 1], F32)
        sb("m13", [128, 1], F32)
        sb("m2", [128, 1], F32)
        sb("var", [128, 1], F32)
        sb("tln", [128, 1], F32)
        sb("rstd", [128, 1], F32)
        sb("nmr", [128, 1], F32)
        sb("accA2", [128, 2], F32)
        sb("accB2", [128, 2], F32)
        sb("dsum", [128, 1], F32)
        sb("dinv", [128, 1], F32)
        sb("dq2", [128, 256], BF16)

        ctx.psA = stack.enter_context(nc.psum_tensor("psA", [128, 2048], F32))
        ctx.psB = stack.enter_context(nc.psum_tensor("psB", [128, 2048], F32))

        ctx.MM = Ctr(stack, nc, "MM")
        ctx.DV = Ctr(stack, nc, "DV")
        ctx.AC = Ctr(stack, nc, "AC")
        ctx.GP = Ctr(stack, nc, "GP")
        ctx.all_ctrs = [ctx.MM, ctx.DV, ctx.AC, ctx.GP]
        ctx.QS = DmaQ(stack, nc, "QS", 6)
        ctx.QG = DmaQ(stack, nc, "QG", 3)
        ctx.QC = DmaQ(stack, nc, "QC", 4)
        ctx.QV = DmaQ(stack, nc, "QV", 3)

        with nc.Block() as block:
            ctx.block = block

            if "prep" in BUILD_PHASES:
                phase_prep(ctx)
                _barrier(ctx)
            if "quant" in BUILD_PHASES:
                phase_quant(ctx)
                _barrier(ctx)
            if "a0" in BUILD_PHASES:
                phase_a0(ctx)
                _barrier(ctx)
            for st in ("aq", "ak", "av"):
                if st in BUILD_PHASES:
                    phase_stage(ctx, st[1])
                    _barrier(ctx)
            if "b" in BUILD_PHASES:
                phase_b(ctx)
                _barrier(ctx)
            if "c" in BUILD_PHASES:
                phase_c(ctx)
                _barrier(ctx)
            if "d" in BUILD_PHASES:
                phase_d(ctx)
            else:
                _barrier(ctx)

    return nc


def kernel(L0_attn_x, L0_attn_Wq, L0_attn_bq, L0_attn_Wk, L0_attn_bk,
           L0_attn_Wv, L0_attn_bv, L0_attn_Wo, L0_attn_bo,
           L0_ffn_w1, L0_ffn_w2, L0_ffn_w3):
    x = np.ascontiguousarray(np.asarray(L0_attn_x, dtype=np.float32))
    args = {
        "wq": np.ascontiguousarray(np.asarray(L0_attn_Wq, np.float32)),
        "wk": np.ascontiguousarray(np.asarray(L0_attn_Wk, np.float32)),
        "wv": np.ascontiguousarray(np.asarray(L0_attn_Wv, np.float32)),
        "wo": np.ascontiguousarray(np.asarray(L0_attn_Wo, np.float32)),
        "bq": np.ascontiguousarray(np.asarray(L0_attn_bq, np.float32).reshape(1, D)),
        "bk": np.ascontiguousarray(np.asarray(L0_attn_bk, np.float32).reshape(1, D)),
        "bv": np.ascontiguousarray(np.asarray(L0_attn_bv, np.float32).reshape(1, D)),
        "bo": np.ascontiguousarray(np.asarray(L0_attn_bo, np.float32).reshape(1, D)),
        "w1": np.ascontiguousarray(np.asarray(L0_ffn_w1, np.float32)),
        "w2": np.ascontiguousarray(np.asarray(L0_ffn_w2, np.float32)),
        "w3": np.ascontiguousarray(np.asarray(L0_ffn_w3, np.float32)),
    }
    nc = build()
    in_maps = []
    for c in range(8):
        shard = np.ascontiguousarray(x[2 * c:2 * c + 2].reshape(NT, D))
        m = {"x": shard}
        m.update(args)
        in_maps.append(m)
    res = run_bass_kernel_spmd(nc, in_maps, core_ids=list(range(8)))
    out = np.concatenate(
        [np.asarray(r["out"]).reshape(NSEQ, T, D) for r in res.results], axis=0)
    return out


# version 13
# speedup vs baseline: 1.0649x; 1.0649x over previous
import contextlib
import numpy as np
import concourse.bass as bass
import concourse.mybir as mybir
from concourse.bass_utils import run_bass_kernel_spmd

F32 = mybir.dt.float32
BF16 = mybir.dt.bfloat16
AF = mybir.ActivationFunctionType
ALU = mybir.AluOpType
AX = mybir.AxisListType

D = 2048
T = 2048
NSEQ = 2
NT = NSEQ * T
NH = 16
HS = 128
HF = 5632
NDC = D // 128
NHC = HF // 128
NTILE = NT // 128
NCH = NT // 512
NUNIT = NSEQ * NH
NW = HF * D
SCALE = float(1.0 / np.sqrt(HS))
SHIFT = 16.0
LNEPS = 1e-5
QEPS = 1e-8
PI = float(np.pi)
TWO_PI = float(2.0 * np.pi)
LN1E4 = float(np.log(10000.0))

ALL_PHASES = ("prep", "quant", "a0", "aq", "ak", "av", "b", "c", "d")
BUILD_PHASES = ALL_PHASES


class Ctr:
    def __init__(self, stack, nc, name):
        self.h = stack.enter_context(nc.semaphore(name))
        self.n = 0

    def inc(self, ins, k=1):
        ins.then_inc(self.h, k)
        self.n += k


class DmaQ:

    def __init__(self, stack, nc, name, k):
        self.k = k
        self.sems = [stack.enter_context(nc.semaphore(f"{name}{i}"))
                     for i in range(k)]
        self.g = 0

    def issue(self, eng, out, in_, **kw):
        g = self.g
        idx, val = g % self.k, 16 * (g // self.k + 1)
        if val > 16:
            eng.wait_ge(self.sems[idx], val - 16)
        eng.dma_start(out=out, in_=in_, **kw).then_inc(self.sems[idx], 16)
        self.g += 1
        return g

    def tok(self, g):
        return (self.sems[g % self.k], 16 * (g // self.k + 1))

    def wait(self, eng, g):
        s, v = self.tok(g)
        eng.wait_ge(s, v)

    def wait_many(self, eng, gs):
        best = {}
        for g in gs:
            idx, val = g % self.k, 16 * (g // self.k + 1)
            if best.get(idx, 0) < val:
                best[idx] = val
        for idx in sorted(best):
            eng.wait_ge(self.sems[idx], best[idx])

    def wait_all(self, eng):
        for idx in range(self.k):
            n = (self.g - idx + self.k - 1) // self.k if self.g > idx else 0
            if n > 0:
                eng.wait_ge(self.sems[idx], 16 * n)


class Ctx:
    pass


def _barrier(ctx):
    snap = [(c, c.n) for c in ctx.all_ctrs if c.n > 0]
    for sec in (ctx.block.sync, ctx.block.tensor, ctx.block.vector,
                ctx.block.scalar, ctx.block.gpsimd):
        @sec
        def _(eng, snap=snap):
            for c, v in snap:
                eng.wait_ge(c.h, v)
            for q in (ctx.QS, ctx.QG, ctx.QC, ctx.QV):
                q.wait_all(eng)


def phase_prep(ctx):
    MM, DV, AC, GP = ctx.MM, ctx.DV, ctx.AC, ctx.GP
    QS = ctx.QS
    b, f = ctx.bf, ctx.f32

    cosb = b(0, 2048)
    sinb = b(2048, 2048)
    cos4 = b(4096, 8192)
    sin4 = b(12288, 8192)
    angf = f(0, 2048)
    cbuf = f(2048, 2048)
    t_ab = f(4096, 2048)
    rowbuf = ctx.ARF[0:1, 6144:8192]
    trow = ctx.ARF[0:1, 8192:10240]

    @ctx.block.gpsimd
    def _(g):
        g.memset(ctx.onesb[:], 1.0)
        g.drain()
        g.affine_select(ctx.eye[:], ctx.onesb[:], [[-1, 128]], ALU.is_equal,
                        0.0, base=0, channel_multiplier=1)
        g.affine_select(ctx.tril[:], ctx.onesb[:], [[-1, 128]], ALU.is_ge,
                        0.0, base=0, channel_multiplier=1)
        g.affine_select(ctx.supe[:].rearrange("p (a c) -> p a c", a=64),
                        ctx.onesb[:].rearrange("p (a c) -> p a c", a=64),
                        [[-2, 64], [-256, 2]], ALU.is_equal, 0.0,
                        base=256, channel_multiplier=1)
        g.affine_select(ctx.subo[:].rearrange("p (a c) -> p a c", a=64),
                        ctx.onesb[:].rearrange("p (a c) -> p a c", a=64),
                        [[-2, 64], [-256, 2]], ALU.is_equal, 0.0,
                        base=-1, channel_multiplier=1)
        g.iota(ctx.erow[0:1, :].rearrange("p (a c) -> p a c", a=64),
               [[1, 64], [0, 2]], channel_multiplier=0,
               allow_small_or_imprecise_dtypes=True)
        g.iota(trow, [[1, 2048]], channel_multiplier=0,
               allow_small_or_imprecise_dtypes=True)
        g.memset(ctx.onesrow[:], 1.0)
        g.memset(ctx.onescol[:], 1.0)
        ins = g.memset(ctx.one11[:], 1.0)
        GP.inc(ins)

    gp_done = GP.n
    mm0, dv0, ac0 = MM.n, DV.n, AC.n
    qs0 = QS.g
    mm_ang = mm0 + 1
    mm_sp = [mm0 + 1 + bi + 1 for bi in range(4)]
    g_row = [qs0 + bi for bi in range(4)]
    dv_ang = dv0 + 1

    @ctx.block.scalar
    def _(s):
        s.wait_ge(GP.h, gp_done)
        ins = s.activation(ctx.invf[:], ctx.erow[:], AF.Exp,
                           scale=np.float32(-(LN1E4 / 64.0)))
        AC.inc(ins)

    ac_invf = AC.n

    @ctx.block.sync
    def _(sync):
        for bi, dsrc in enumerate((ctx.bq_d, ctx.bk_d, ctx.bv_d, ctx.bo_d)):
            if bi >= 1:
                sync.wait_ge(MM.h, mm_sp[bi - 1])
            QS.issue(sync, rowbuf, dsrc[0:1, :])

    @ctx.block.tensor
    def _(t):
        t.wait_ge(AC.h, ac_invf)
        t.wait_ge(GP.h, gp_done)
        for j in range(4):
            ins = t.matmul(ctx.psA[:, j * 512:(j + 1) * 512], ctx.invf[:],
                           trow[:, j * 512:(j + 1) * 512],
                           start=True, stop=True)
        MM.inc(ins)
        for bi in range(4):
            QS.wait(t, g_row[bi])
            if bi < 3:
                for dc in range(NDC):
                    ins = t.matmul(ctx.psB[:, bi * 16 + dc:bi * 16 + dc + 1],
                                   rowbuf[:, dc * 128:(dc + 1) * 128],
                                   ctx.one11[:], start=True, stop=True)
                MM.inc(ins)
            else:
                t.wait_ge(DV.h, dv_ang)
                for j in range(4):
                    ins = t.matmul(ctx.psA[:, j * 512:(j + 1) * 512],
                                   ctx.onesrow[:],
                                   rowbuf[:, j * 512:(j + 1) * 512],
                                   start=True, stop=True)
                MM.inc(ins)

    @ctx.block.vector
    def _(v):
        v.wait_ge(MM.h, mm_ang)
        v.tensor_copy(angf, ctx.psA[:, 0:2048])
        v.drain()
        steps = [(TWO_PI * (2 ** k), TWO_PI * (2 ** k))
                 for k in range(8, -1, -1)]
        steps.append((PI, TWO_PI))
        for thr, sub in steps:
            v.tensor_scalar(cbuf, angf, float(thr), None, ALU.is_ge)
            v.drain()
            v.scalar_tensor_tensor(angf, cbuf, float(-sub), angf,
                                   ALU.mult, ALU.add)
            v.drain()
        ins = v.tensor_scalar(angf, angf, PI, -PI, ALU.min, ALU.max)
        v.drain()
        DV.inc(ins)
        v.wait_ge(MM.h, mm_sp[2])
        v.tensor_scalar(ctx.bq_t[:], ctx.psB[:, 0:16], SCALE, None, ALU.mult)
        v.tensor_copy(ctx.bk_t[:], ctx.psB[:, 16:32])
        ins = v.tensor_copy(ctx.bv_t[:], ctx.psB[:, 32:48])
        DV.inc(ins)
        v.wait_ge(MM.h, mm_sp[3])
        ins = v.tensor_copy(ctx.bo_bc[:], ctx.psA[:, 0:2048])
        DV.inc(ins)

    @ctx.block.scalar
    def _(s):
        s.wait_ge(DV.h, dv_ang)
        s.activation(sinb, angf, AF.Sin)
        s.activation(t_ab, angf, AF.Abs)
        s.drain()
        ins = s.activation(cosb, t_ab, AF.Sin, scale=np.float32(-1.0),
                           bias=np.float32(PI / 2.0))
        s.drain()
        AC.inc(ins)

    ac_cos = AC.n

    @ctx.block.vector
    def _(v):
        v.wait_ge(GP.h, gp_done)
        ins = v.tensor_sub(ctx.mshuf[:], ctx.supe[:], ctx.subo[:])
        DV.inc(ins)
        v.wait_ge(AC.h, ac_cos)
        for q in range(4):
            for r in range(4):
                o = q * 2048 + r * 512
                v.tensor_copy(cos4[:, o:o + 512], cosb[:, q * 512:(q + 1) * 512])
                ins = v.tensor_copy(sin4[:, o:o + 512],
                                    sinb[:, q * 512:(q + 1) * 512])
        DV.inc(ins)


def phase_quant(ctx):
    MM, DV, AC = ctx.MM, ctx.DV, ctx.AC
    QS, QV, QG = ctx.QS, ctx.QV, ctx.QG
    b, f = ctx.bf, ctx.f32

    CW = 2816
    slab4 = f(0, 4 * CW)
    tmpA = b(20480, CW)
    tmpB = b(26112, CW)
    qbf4 = b(31744, 4 * CW)
    junkB = b(43008, 4 * CW)
    stg2 = b(54272, 4096)

    ws = [(ctx.w1_d, ctx.q1T_scr, NHC, D, True),
          (ctx.w2_d, ctx.q2T_scr, NHC, D, True),
          (ctx.w3_d, ctx.q3n_scr, 2 * NDC, CW, False)]

    def wsrc(w_d, do_T, s):
        if do_T:
            return w_d[s * 128:(s + 1) * 128, :]
        return w_d[(s // 2) * 128:(s // 2) * 128 + 128,
                   (s % 2) * CW:(s % 2) * CW + CW]

    def slot(s, ncol):
        return slab4[:, (s % 4) * CW:(s % 4) * CW + ncol]

    cnt = {0: QS.g, 1: QV.g}
    ac = AC.n
    dv = DV.n
    mm = MM.n
    qc = QG.g
    tb_par = 0
    P = {}
    for wi, (w_d, scr, nslab, ncol, do_T) in enumerate(ws):
        d = {}

        def tk(s):
            p = s % 2
            g = cnt[p]
            cnt[p] += 1
            return (p, g)

        d["g_p1"] = [tk(s) for s in range(nslab)]
        d["g_p2"] = [tk(s) for s in range(nslab)]
        d["ac_red"] = [ac + s + 1 for s in range(nslab)]
        ac += nslab
        dv += 1
        d["dv_gtot"] = dv
        dv += 1
        d["dv_thr"] = dv
        dv += 1
        d["dv_bc"] = dv
        mm += 1
        d["mm_gsum"] = mm
        mm += 1
        d["mm_rbc"] = mm
        d["dv_q"] = []
        d["dv_tb"] = []
        d["mm_tb"] = []
        d["tb_par"] = []
        for s in range(nslab):
            dv += 1
            d["dv_q"].append(dv)
            if do_T:
                row_dv, row_mm, row_par = [], [], []
                for bb in range(4):
                    mm += 1
                    row_mm.append(mm)
                    dv += 1
                    row_dv.append(dv)
                    row_par.append(tb_par)
                    tb_par += 1
                d["dv_tb"].append(row_dv)
                d["mm_tb"].append(row_mm)
                d["tb_par"].append(row_par)
        d["g_st"] = [qc + s for s in range(nslab)]
        qc += nslab
        P[wi] = d

    QMAP = {0: QS, 1: QV}

    def wait_tok(eng, t):
        QMAP[t[0]].wait(eng, t[1])

    tb_all = []
    for wi in (0, 1):
        for s in range(NHC):
            for bb in range(4):
                tb_all.append((wi, s, bb))

    @ctx.block.sync
    def _(sync):
        for wi, (w_d, scr, nslab, ncol, do_T) in enumerate(ws):
            d = P[wi]
            for s in range(0, nslab, 2):
                if s >= 4:
                    sync.wait_ge(AC.h, d["ac_red"][s - 4])
                elif wi > 0:
                    pd = P[wi - 1]
                    pn = len(pd["dv_q"])
                    sync.wait_ge(DV.h, pd["dv_q"][pn - 4 + s])
                assert (0, QS.g) == d["g_p1"][s]
                QS.issue(sync, slot(s, ncol), wsrc(w_d, do_T, s))
            for s in range(0, nslab, 2):
                if s >= 4:
                    sync.wait_ge(DV.h, d["dv_q"][s - 4])
                else:
                    sync.wait_ge(AC.h, d["ac_red"][nslab - 4 + s])
                assert (0, QS.g) == d["g_p2"][s]
                QS.issue(sync, slot(s, ncol), wsrc(w_d, do_T, s))

    @ctx.block.scalar
    def _(s_):
        for wi, (w_d, scr, nslab, ncol, do_T) in enumerate(ws):
            d = P[wi]
            for s in (1, 3):
                if wi > 0:
                    pd = P[wi - 1]
                    pn = len(pd["dv_q"])
                    s_.wait_ge(DV.h, pd["dv_q"][pn - 4 + s])
                assert (1, QV.g) == d["g_p1"][s]
                QV.issue(s_, slot(s, ncol), wsrc(w_d, do_T, s))
            for s in range(nslab):
                if s == 0 and wi > 0:
                    s_.wait_ge(DV.h, P[wi - 1]["dv_gtot"])
                wait_tok(s_, d["g_p1"][s])
                ins = s_.activation(junkB[:, (s % 4) * CW:(s % 4) * CW + ncol],
                                    slot(s, ncol), AF.Abs,
                                    accum_out=ctx.gcols[:, s:s + 1])
                AC.inc(ins)
                if s % 2 == 1 and s + 4 < nslab:
                    assert (1, QV.g) == d["g_p1"][s + 4]
                    QV.issue(s_, slot(s + 4, ncol), wsrc(w_d, do_T, s + 4))
            for s in range(1, nslab, 2):
                if s >= 4:
                    s_.wait_ge(DV.h, d["dv_q"][s - 4])
                assert (1, QV.g) == d["g_p2"][s]
                QV.issue(s_, slot(s, ncol), wsrc(w_d, do_T, s))

    @ctx.block.vector
    def _(v):
        for wi, (w_d, scr, nslab, ncol, do_T) in enumerate(ws):
            d = P[wi]
            v.wait_ge(AC.h, d["ac_red"][nslab - 1])
            ins = v.reduce_sum(ctx.gtot[:], ctx.gcols[:, 0:nslab], axis=AX.X)
            v.drain()
            DV.inc(ins)
            v.wait_ge(MM.h, d["mm_gsum"])
            ins = v.tensor_scalar(ctx.thr11[:], ctx.psB[:1, 48:49],
                                  0.5 / float(NW), 0.5 * QEPS, ALU.mult,
                                  ALU.add)
            v.drain()
            DV.inc(ins)
            v.wait_ge(MM.h, d["mm_rbc"])
            v.tensor_copy(ctx.thrbc[:], ctx.psB[:, 49:50])
            v.drain()
            ins = v.tensor_scalar(ctx.nthrbc[:], ctx.thrbc[:], -1.0, None,
                                  ALU.mult)
            v.drain()
            DV.inc(ins)
            for s in range(nslab):
                wait_tok(v, d["g_p2"][s])
                qb = qbf4[:, (s % 4) * CW:(s % 4) * CW + ncol]
                if do_T and s >= 4:
                    v.wait_ge(MM.h, d["mm_tb"][s - 4][3])
                if (not do_T) and s >= 4:
                    QG.wait(v, d["g_st"][s - 4])
                if wi > 0 and s < 4:
                    pd = P[wi - 1]
                    pn = len(pd["dv_q"])
                    if ws[wi - 1][4]:
                        v.wait_ge(MM.h, pd["mm_tb"][pn - 4 + s][3])
                    else:
                        QG.wait(v, pd["g_st"][pn - 4 + s])
                sl = slot(s, ncol)
                v.tensor_scalar(tmpA[:, 0:ncol], sl, ctx.thrbc[:, 0:1], None,
                                ALU.is_gt)
                v.tensor_scalar(tmpB[:, 0:ncol], sl, ctx.nthrbc[:, 0:1], None,
                                ALU.is_lt)
                v.drain()
                ins = v.tensor_sub(qb, tmpA[:, 0:ncol], tmpB[:, 0:ncol])
                v.drain()
                DV.inc(ins)
                if do_T:
                    stg = stg2[:, (s % 2) * 2048:(s % 2) * 2048 + 2048]
                    for bb in range(4):
                        v.wait_ge(MM.h, d["mm_tb"][s][bb])
                        if bb == 0:
                            if s >= 2:
                                QG.wait(v, d["g_st"][s - 2])
                            elif wi > 0:
                                pd = P[wi - 1]
                                QG.wait(v, pd["g_st"][len(pd["g_st"]) - 2 + s])
                        par = d["tb_par"][s][bb]
                        reg = ctx.psA if (par % 2 == 0) else ctx.psB
                        ins = v.tensor_copy(stg[:, bb * 512:(bb + 1) * 512],
                                            reg[:, 0:512])
                        DV.inc(ins)

    @ctx.block.tensor
    def _(t):
        for wi, (w_d, scr, nslab, ncol, do_T) in enumerate(ws):
            d = P[wi]
            t.wait_ge(DV.h, d["dv_gtot"])
            ins = t.matmul(ctx.psB[:1, 48:49], ctx.gtot[:], ctx.onescol[:],
                           start=True, stop=True)
            MM.inc(ins)
            t.wait_ge(DV.h, d["dv_thr"])
            ins = t.matmul(ctx.psB[:, 49:50], ctx.onesrow[:], ctx.thr11[:],
                           start=True, stop=True)
            MM.inc(ins)
            if do_T:
                for s in range(nslab):
                    t.wait_ge(DV.h, d["dv_q"][s])
                    qb = qbf4[:, (s % 4) * CW:(s % 4) * CW + ncol]
                    for bb in range(4):
                        par = d["tb_par"][s][bb]
                        if par >= 2:
                            pw, psl, pbb = tb_all[par - 2]
                            t.wait_ge(DV.h, P[pw]["dv_tb"][psl][pbb])
                        reg = ctx.psA if (par % 2 == 0) else ctx.psB
                        for j in range(4):
                            dc = bb * 4 + j
                            ins = t.matmul(reg[:, j * 128:(j + 1) * 128],
                                           qb[:, dc * 128:(dc + 1) * 128],
                                           ctx.eye[:], start=True, stop=True)
                        MM.inc(ins)

    @ctx.block.gpsimd
    def _(g_):
        for wi, (w_d, scr, nslab, ncol, do_T) in enumerate(ws):
            d = P[wi]
            for s in range(nslab):
                if do_T:
                    g_.wait_ge(DV.h, d["dv_tb"][s][3])
                    stg = stg2[:, (s % 2) * 2048:(s % 2) * 2048 + 2048]
                    assert QG.g == d["g_st"][s]
                    QG.issue(g_, scr[s], stg)
                else:
                    g_.wait_ge(DV.h, d["dv_q"][s])
                    qb = qbf4[:, (s % 4) * CW:(s % 4) * CW + ncol]
                    assert QG.g == d["g_st"][s]
                    QG.issue(g_, scr[(s // 2) * 128:(s // 2) * 128 + 128,
                                     (s % 2) * CW:(s % 2) * CW + CW], qb)


def phase_a0(ctx):
    MM, DV, AC = ctx.MM, ctx.DV, ctx.AC
    QS, QG = ctx.QS, ctx.QG
    b, f = ctx.bf, ctx.f32
    mm0, dv0, ac0 = MM.n, DV.n, AC.n
    qs0, qg0 = QS.g, QG.g

    xt2 = f(0, 4 * D)
    junk = f(4 * D, D)
    junkA = f(5 * D, D)
    xnbf = b(20480, D)
    stg = b(20480 + D, 8192)

    def dvt(i):
        return dv0 + 7 * i

    def act(i):
        return ac0 + 3 * i

    def mmt(i):
        return mm0 + 4 * i

    g_x = [qs0 + i for i in range(NTILE)]
    g_st = [qg0 + c for c in range(NCH)]

    @ctx.block.sync
    def _(sync):
        for i in range(NTILE):
            if i >= 4:
                sync.wait_ge(AC.h, act(i - 4) + 3)
            QS.issue(sync, xt2[:, (i % 4) * D:(i % 4) * D + D],
                     ctx.x_d[i * 128:(i + 1) * 128, :])

    @ctx.block.vector
    def _(v):
        for i in range(NTILE):
            xt = xt2[:, (i % 4) * D:(i % 4) * D + D]
            QS.wait(v, g_x[i])
            v.tensor_scalar(junk, xt, 1.0, None, ALU.mult, ALU.add,
                            accum_out=ctx.sx[:])
            v.drain()
            ins = v.tensor_scalar(ctx.m1[:], ctx.sx[:], 1.0 / D, None, ALU.mult)
            v.drain()
            DV.inc(ins)
            v.wait_ge(AC.h, act(i) + 1)
            v.tensor_mul(ctx.m2[:], ctx.m1[:], ctx.m1[:])
            v.drain()
            ins = v.scalar_tensor_tensor(ctx.var[:], ctx.sxx[:], 1.0 / D,
                                         ctx.m2[:], ALU.mult, ALU.subtract)
            v.drain()
            DV.inc(ins)
            v.wait_ge(AC.h, act(i) + 2)
            ins = v.scalar_tensor_tensor(ctx.nmr[:], ctx.m1[:], -1.0,
                                         ctx.rstd[:], ALU.mult, ALU.mult)
            v.drain()
            DV.inc(ins)
            j = i % 4
            for bb in range(4):
                v.wait_ge(MM.h, mmt(i) + bb + 1)
                if j == 0 and bb == 0 and i >= 4:
                    QG.wait(v, g_st[i // 4 - 1])
                reg = ctx.psA if (bb % 2 == 0) else ctx.psB
                dst = stg.rearrange("p (dc c) -> p dc c", dc=16)
                dst = dst[:, bb * 4:(bb + 1) * 4, j * 128:(j + 1) * 128]
                src = reg[:, 0:512].rearrange("p (dc c) -> p dc c", dc=4)
                ins = v.tensor_copy(dst, src)
                DV.inc(ins)

    @ctx.block.scalar
    def _(s):
        for i in range(NTILE):
            xt = xt2[:, (i % 4) * D:(i % 4) * D + D]
            QS.wait(s, g_x[i])
            if i >= 1:
                s.wait_ge(DV.h, dvt(i - 1) + 2)
            ins = s.activation(junkA, xt, AF.Square, accum_out=ctx.sxx[:])
            AC.inc(ins)
            s.wait_ge(DV.h, dvt(i) + 2)
            s.activation(ctx.tln[:], ctx.var[:], AF.Ln, bias=np.float32(LNEPS))
            s.drain()
            ins = s.activation(ctx.rstd[:], ctx.tln[:], AF.Exp,
                               scale=np.float32(-0.5))
            s.drain()
            AC.inc(ins)
            s.wait_ge(DV.h, dvt(i) + 3)
            if i >= 1:
                s.wait_ge(MM.h, mmt(i))
            ins = s.activation(xnbf, xt, AF.Identity, scale=ctx.rstd[:, 0:1],
                               bias=ctx.nmr[:, 0:1])
            s.drain()
            AC.inc(ins)

    @ctx.block.tensor
    def _(t):
        for i in range(NTILE):
            t.wait_ge(AC.h, act(i) + 3)
            for bb in range(4):
                g = 4 * i + bb
                if g >= 2:
                    pb = g - 2
                    t.wait_ge(DV.h, dvt(pb // 4) + 3 + (pb % 4) + 1)
                reg = ctx.psA if (bb % 2 == 0) else ctx.psB
                for j in range(4):
                    dc = bb * 4 + j
                    ins = t.matmul(reg[:, j * 128:(j + 1) * 128],
                                   xnbf[:, dc * 128:(dc + 1) * 128],
                                   ctx.eye[:], start=True, stop=True)
                MM.inc(ins)

    @ctx.block.gpsimd
    def _(g):
        for c in range(NCH):
            g.wait_ge(DV.h, dvt(4 * c + 3) + 7)
            QG.issue(g, ctx.xnT_scr[c], stg)

    assert MM.n == mm0 + 4 * NTILE
    assert DV.n == dv0 + 7 * NTILE
    assert AC.n == ac0 + 3 * NTILE


def phase_stage(ctx, which):
    MM, DV, AC = ctx.MM, ctx.DV, ctx.AC
    QS, QG = ctx.QS, ctx.QG
    b = ctx.bf
    rope = which in ("q", "k")
    w_d = {"q": ctx.wq_d, "k": ctx.wk_d, "v": ctx.wv_d}[which]
    scr = {"q": ctx.q_scr, "k": ctx.k_scr, "v": ctx.v_scr}[which]
    bias_t = {"q": ctx.bq_t, "k": ctx.bk_t, "v": ctx.bv_t}[which]
    evac_scale = SCALE if which == "q" else 1.0

    cos4 = b(4096, 8192)
    sin4 = b(12288, 8192)
    wg2 = b(20480, 16384)
    xnt2 = b(36864, 16384)
    qg2 = b(53248, 4096)
    shq2 = b(57344, 4096)
    t1 = b(61440, 2048)
    t2 = b(63488, 2048)
    rot2 = b(65536, 4096)

    mm0, dv0, ac0 = MM.n, DV.n, AC.n
    qs0, qg0 = QS.g, QG.g
    NI = 32
    kmm = 2 if rope else 1
    kac = 2 if rope else 1

    def mm_proj(it):
        return mm0 + kmm * it + 1

    def mm_all(it):
        return mm0 + kmm * it + kmm

    def ac_evac(it):
        return ac0 + kac * it + 1

    def ac_all(it):
        return ac0 + kac * it + kac

    def dv_rot(it):
        return dv0 + it + 1

    def g_store(it, hl):
        return (qg0 + 2 + 4 * it + hl + (1 if it > 7 else 0)
                + (1 if it > 15 else 0))

    def g_wload(hg):
        return [qg0, qg0 + 1, qg0 + 34, qg0 + 67][hg]

    g_xnt = [qs0 + it for it in range(NI)]

    def _wload(g, hg):
        if hg >= 2:
            g.wait_ge(MM.h, mm_proj((hg - 2) * 8 + 7))
        wsrc = w_d[:, hg * 512:(hg + 1) * 512]
        wsrc = wsrc.rearrange("(dc p) n -> p dc n", p=128)
        wdst = wg2[:, (hg % 2) * 8192:(hg % 2) * 8192 + 8192]
        wdst = wdst.rearrange("p (dc n) -> p dc n", dc=16)
        assert QG.g == g_wload(hg), (QG.g, g_wload(hg))
        QG.issue(g, wdst, wsrc)

    @ctx.block.gpsimd
    def _(g):
        _wload(g, 0)
        _wload(g, 1)
        for it in range(NI):
            hg, tc = it // 8, it % 8
            seq, tl = tc // 4, (tc % 4) * 512
            if rope:
                g.wait_ge(DV.h, dv_rot(it))
                stsrc = rot2[:, (it % 2) * 2048:(it % 2) * 2048 + 2048]
            else:
                g.wait_ge(AC.h, ac_evac(it))
                stsrc = qg2[:, (it % 2) * 2048:(it % 2) * 2048 + 2048]
            for hl in range(4):
                u = seq * NH + hg * 4 + hl
                assert QG.g == g_store(it, hl)
                QG.issue(g, scr[u, :, tl:tl + 512],
                         stsrc[:, hl * 512:(hl + 1) * 512])
            if it == 7:
                _wload(g, 2)
            if it == 15:
                _wload(g, 3)

    @ctx.block.sync
    def _(sync):
        for it in range(NI):
            if it >= 2:
                sync.wait_ge(MM.h, mm_proj(it - 2))
            tc = it % 8
            QS.issue(sync, xnt2[:, (it % 2) * 8192:(it % 2) * 8192 + 8192],
                     ctx.xnT_scr[tc])

    @ctx.block.tensor
    def _(t):
        for it in range(NI):
            hg = it // 8
            QS.wait(t, g_xnt[it])
            QG.wait(t, g_wload(hg))
            if it >= 2:
                t.wait_ge(AC.h, ac_all(it - 2))
            reg = ctx.psA if (it % 2 == 0) else ctx.psB
            wg = wg2[:, (hg % 2) * 8192:(hg % 2) * 8192 + 8192]
            xn = xnt2[:, (it % 2) * 8192:(it % 2) * 8192 + 8192]
            for hl in range(4):
                for dc in range(NDC):
                    ins = t.matmul(
                        reg[:, hl * 512:(hl + 1) * 512],
                        wg[:, dc * 512 + hl * 128:dc * 512 + hl * 128 + 128],
                        xn[:, dc * 512:(dc + 1) * 512],
                        start=(dc == 0), stop=(dc == NDC - 1))
            MM.inc(ins)
            if rope:
                t.wait_ge(AC.h, ac_evac(it))
                qg = qg2[:, (it % 2) * 2048:(it % 2) * 2048 + 2048]
                for hl in range(4):
                    ins = t.matmul(reg[:, hl * 512:(hl + 1) * 512],
                                   ctx.mshuf[:], qg[:, hl * 512:(hl + 1) * 512],
                                   start=True, stop=True)
                MM.inc(ins)

    @ctx.block.scalar
    def _(s):
        for it in range(NI):
            hg = it // 8
            reg = ctx.psA if (it % 2 == 0) else ctx.psB
            qg = qg2[:, (it % 2) * 2048:(it % 2) * 2048 + 2048]
            s.wait_ge(MM.h, mm_proj(it))
            if it >= 2:
                if rope:
                    s.wait_ge(DV.h, dv_rot(it - 2))
                else:
                    QG.wait_many(s, [g_store(it - 2, hl) for hl in range(4)])
            for hl in range(4):
                h = hg * 4 + hl
                ins = s.activation(qg[:, hl * 512:(hl + 1) * 512],
                                   reg[:, hl * 512:(hl + 1) * 512],
                                   AF.Identity, scale=np.float32(evac_scale),
                                   bias=bias_t[:, h:h + 1])
            AC.inc(ins)
            if rope:
                s.wait_ge(MM.h, mm_all(it))
                shq = shq2[:, (it % 2) * 2048:(it % 2) * 2048 + 2048]
                if it >= 2:
                    s.wait_ge(DV.h, dv_rot(it - 2))
                ins = s.activation(shq, reg[:, 0:2048], AF.Identity)
                s.drain()
                AC.inc(ins)

    if rope:
        @ctx.block.vector
        def _(v):
            for it in range(NI):
                tc = it % 8
                v.wait_ge(AC.h, ac_all(it))
                if it >= 2:
                    QG.wait_many(v, [g_store(it - 2, hl) for hl in range(4)])
                qg = qg2[:, (it % 2) * 2048:(it % 2) * 2048 + 2048]
                shq = shq2[:, (it % 2) * 2048:(it % 2) * 2048 + 2048]
                coff = (tc % 4) * 2048
                v.tensor_mul(t1, qg, cos4[:, coff:coff + 2048])
                v.tensor_mul(t2, shq, sin4[:, coff:coff + 2048])
                v.drain()
                rot = rot2[:, (it % 2) * 2048:(it % 2) * 2048 + 2048]
                ins = v.tensor_add(rot, t1, t2)
                v.drain()
                DV.inc(ins)

    assert MM.n == mm0 + kmm * NI
    assert AC.n == ac0 + kac * NI
    if rope:
        assert DV.n == dv0 + NI


def phase_b(ctx):
    MM, DV, AC = ctx.MM, ctx.DV, ctx.AC
    QS = ctx.QS
    b = ctx.bf

    qkvb = b(20480, 12288)
    vsb = b(32768, 2048)
    psb2 = b(34816, 4096)
    ptsb2 = b(38912, 4096)
    otsb2 = b(43008, 4096)

    mm0, dv0, ac0 = MM.n, DV.n, AC.n
    qs0 = QS.g
    nb = [(qt // 4) + 1 for qt in range(16)]

    mm_vt, mm_s, mm_tb, mm_pv = {}, {}, {}, {}
    mm = mm0
    tb_seq = []
    tb_gidx = {}
    for u in range(NUNIT):
        mm += 1
        mm_vt[u] = mm
        for step in range(17):
            if step < 16:
                mm += 1
                mm_s[(u, step, 0)] = mm
                mm += 1
                mm_s[(u, step, 1)] = mm
            if step >= 1:
                qt = step - 1
                for bb in range(nb[qt]):
                    mm += 1
                    mm_tb[(u, qt, bb)] = mm
                    tb_gidx[(u, qt, bb)] = len(tb_seq)
                    tb_seq.append((u, qt, bb))
                    if bb >= 2:
                        mm += 1
                        mm_pv[(u, qt, bb - 2)] = mm
                for bb2 in range(max(0, nb[qt] - 2), nb[qt]):
                    mm += 1
                    mm_pv[(u, qt, bb2)] = mm
    mm_end = mm

    ac_e = {}
    ac = ac0
    for u in range(NUNIT):
        for qt in range(16):
            ac += 1
            ac_e[(u, qt, 0)] = ac
            ac += 1
            ac_e[(u, qt, 1)] = ac
    ac_end = ac

    dv_vt, dv_dD, dv_pte, dv_ot = {}, {}, {}, {}
    dv = dv0
    for u in range(NUNIT):
        dv += 4
        dv_vt[u] = dv
        for step in range(17):
            if step >= 1:
                qt = step - 1
                for bb in range(nb[qt]):
                    dv += 1
                    dv_pte[(u, qt, bb)] = dv
                dv += 1
                dv_ot[(u, qt)] = dv
            if step < 16:
                dv += 1
                dv += 1
                dv_dD[(u, step)] = dv
    dv_end = dv

    def g_load(u, j):
        return qs0 + 4 * u + j - (1 if u >= 1 else 0)

    def g_ostore(u):
        if u < NUNIT - 1:
            return qs0 + 4 * (u + 1) + 2
        return qs0 + 4 * (NUNIT - 1) + 3

    def ubufs(u):
        o = (u % 2) * 6144
        return (qkvb[:, o:o + 2048], qkvb[:, o + 2048:o + 4096],
                qkvb[:, o + 4096:o + 6144])

    def prev_same_qt(u, qt):
        if qt >= 2:
            return (u, qt - 2)
        if u >= 1:
            return (u - 1, 14 + qt)
        return None

    @ctx.block.sync
    def _(sync):
        for u in range(NUNIT):
            if u >= 2:
                sync.wait_ge(MM.h, mm_s[(u - 2, 15, 1)])
            qb, kb, vb = ubufs(u)
            assert QS.g == g_load(u, 0)
            QS.issue(sync, qb, ctx.q_scr[u])
            QS.issue(sync, kb, ctx.k_scr[u])
            QS.issue(sync, vb, ctx.v_scr[u])
            if u >= 1:
                sync.wait_ge(DV.h, dv_ot[(u - 1, 15)])
                assert QS.g == g_ostore(u - 1)
                QS.issue(sync, ctx.oT_scr[u - 1],
                         otsb2[:, ((u - 1) % 2) * 2048:
                               ((u - 1) % 2) * 2048 + 2048])
        sync.wait_ge(DV.h, dv_ot[(NUNIT - 1, 15)])
        assert QS.g == g_ostore(NUNIT - 1)
        QS.issue(sync, ctx.oT_scr[NUNIT - 1],
                 otsb2[:, ((NUNIT - 1) % 2) * 2048:
                       ((NUNIT - 1) % 2) * 2048 + 2048])

    @ctx.block.tensor
    def _(t):
        for u in range(NUNIT):
            qb, kb, vb = ubufs(u)
            QS.wait_many(t, [g_load(u, j) for j in range(3)])
            if u >= 1:
                t.wait_ge(AC.h, ac_e[(u - 1, 15, 1)])
            for kt in range(16):
                ins = t.matmul(ctx.psA[:, kt * 128:(kt + 1) * 128],
                               vb[:, kt * 128:(kt + 1) * 128], ctx.eye[:],
                               start=True, stop=True)
            MM.inc(ins)
            for step in range(17):
                if step < 16:
                    qt = step
                    lhs = qb[:, qt * 128:(qt + 1) * 128]
                    for h in range(2):
                        if qt == 0:
                            t.wait_ge(DV.h, dv_vt[u])
                        else:
                            t.wait_ge(AC.h, ac_e[(u, qt - 1, h)])
                        for kk in range(2 * h, 2 * h + 2):
                            ins = t.matmul(ctx.psA[:, kk * 512:(kk + 1) * 512],
                                           lhs, kb[:, kk * 512:(kk + 1) * 512],
                                           start=True, stop=True)
                        MM.inc(ins)
                if step >= 1:
                    qt = step - 1
                    t.wait_ge(DV.h, dv_dD[(u, qt)])
                    psb = psb2[:, (qt % 2) * 2048:(qt % 2) * 2048 + 2048]
                    dq = ctx.dq2[:, (qt % 2) * 128:(qt % 2) * 128 + 128]
                    pts = ptsb2[:, (qt % 2) * 2048:(qt % 2) * 2048 + 2048]

                    def pv_partial(bb2, qt=qt, u=u, pts=pts, t=t):
                        t.wait_ge(DV.h, dv_pte[(u, qt, bb2)])
                        if bb2 == 0 and (u, qt) != (0, 0):
                            pu, pq = (u, qt - 1) if qt > 0 else (u - 1, 15)
                            t.wait_ge(DV.h, dv_ot[(pu, pq)])
                        lo = bb2 * 4
                        hi = min(qt, lo + 3)
                        for kt in range(lo, hi + 1):
                            ins = t.matmul(ctx.psB[:, 1024:1152],
                                           vsb[:, kt * 128:(kt + 1) * 128],
                                           pts[:, kt * 128:(kt + 1) * 128],
                                           start=(kt == 0), stop=(kt == qt))
                        MM.inc(ins)

                    for bb in range(nb[qt]):
                        g = tb_gidx[(u, qt, bb)]
                        if g >= 3:
                            t.wait_ge(DV.h, dv_pte[tb_seq[g - 3]])
                        reg = (ctx.psB[:, 0:512], ctx.psB[:, 512:1024],
                               ctx.psB[:, 1536:2048])[g % 3]
                        for j in range(4):
                            kt = bb * 4 + j
                            if kt > qt:
                                break
                            ins = t.matmul(reg[:, j * 128:(j + 1) * 128],
                                           psb[:, kt * 128:(kt + 1) * 128],
                                           dq, start=True, stop=True)
                        MM.inc(ins)
                        if bb >= 2:
                            pv_partial(bb - 2)
                    for bb2 in range(max(0, nb[qt] - 2), nb[qt]):
                        pv_partial(bb2)

    @ctx.block.scalar
    def _(s):
        for u in range(NUNIT):
            for qt in range(16):
                psb = psb2[:, (qt % 2) * 2048:(qt % 2) * 2048 + 2048]
                pv = prev_same_qt(u, qt)
                for h in range(2):
                    acc = (ctx.accA2 if h == 0 else ctx.accB2)[:, (qt % 2):(qt % 2) + 1]
                    if pv is not None:
                        if h == 0:
                            s.wait_ge(MM.h, mm_tb[(pv[0], pv[1], nb[pv[1]] - 1)])
                        s.wait_ge(DV.h, dv_dD[pv])
                    s.wait_ge(MM.h, mm_s[(u, qt, h)])
                    ins = s.activation(psb[:, h * 1024:h * 1024 + 1024],
                                       ctx.psA[:, h * 1024:h * 1024 + 1024],
                                       AF.Exp, bias=np.float32(-SHIFT),
                                       accum_out=acc)
                    AC.inc(ins)

    @ctx.block.vector
    def _(v):
        for u in range(NUNIT):
            for bb in range(4):
                v.wait_ge(MM.h, mm_vt[u])
                ins = v.tensor_copy(vsb[:, bb * 512:(bb + 1) * 512],
                                    ctx.psA[:, bb * 512:(bb + 1) * 512])
                DV.inc(ins)
            for step in range(17):
                if step >= 1:
                    qt = step - 1
                    pts = ptsb2[:, (qt % 2) * 2048:(qt % 2) * 2048 + 2048]
                    for bb in range(nb[qt]):
                        v.wait_ge(MM.h, mm_tb[(u, qt, bb)])
                        nblk = min(4, qt + 1 - bb * 4)
                        g = tb_gidx[(u, qt, bb)]
                        reg = (ctx.psB[:, 0:512], ctx.psB[:, 512:1024],
                               ctx.psB[:, 1536:2048])[g % 3]
                        ins = v.tensor_copy(
                            pts[:, bb * 512:bb * 512 + nblk * 128],
                            reg[:, 0:nblk * 128])
                        DV.inc(ins)
                    v.wait_ge(MM.h, mm_pv[(u, qt, nb[qt] - 1)])
                    if qt == 0 and u >= 2:
                        QS.wait(v, g_ostore(u - 2))
                    ot = otsb2[:, (u % 2) * 2048:(u % 2) * 2048 + 2048]
                    ins = v.tensor_copy(ot[:, qt * 128:(qt + 1) * 128],
                                        ctx.psB[:, 1024:1152])
                    DV.inc(ins)
                if step < 16:
                    qt = step
                    psb = psb2[:, (qt % 2) * 2048:(qt % 2) * 2048 + 2048]
                    dq = ctx.dq2[:, (qt % 2) * 128:(qt % 2) * 128 + 128]
                    accA = ctx.accA2[:, (qt % 2):(qt % 2) + 1]
                    accB = ctx.accB2[:, (qt % 2):(qt % 2) + 1]
                    v.wait_ge(AC.h, ac_e[(u, qt, 1)])
                    pv = prev_same_qt(u, qt)
                    if pv is not None:
                        v.wait_ge(MM.h, mm_tb[(pv[0], pv[1], nb[pv[1]] - 1)])
                    ins = v.tensor_mul(psb[:, qt * 128:(qt + 1) * 128],
                                       psb[:, qt * 128:(qt + 1) * 128],
                                       ctx.tril[:])
                    DV.inc(ins)
                    v.tensor_add(ctx.dsum[:], accA, accB)
                    v.drain()
                    v.reciprocal(ctx.dinv[:], ctx.dsum[:])
                    v.drain()
                    ins = v.tensor_scalar(dq, ctx.eye[:], ctx.dinv[:, 0:1],
                                          None, ALU.mult)
                    v.drain()
                    DV.inc(ins)

    assert MM.n == mm_end
    assert AC.n == ac_end
    assert DV.n == dv_end


def phase_c(ctx):
    MM, DV, AC = ctx.MM, ctx.DV, ctx.AC
    QS, QG, QC = ctx.QS, ctx.QG, ctx.QC
    b, f = ctx.bf, ctx.f32

    wo = b(0, 32768)
    ot2 = b(32768, 16384)
    xn2bf = b(49152, 2048)
    stg = b(51200, 8192)
    xt2 = f(0, 2 * D)
    ysb = f(2 * D, D)
    junk = f(3 * D, D)
    xpo2 = f(4 * D, 2 * D)

    mm0, dv0, ac0 = MM.n, DV.n, AC.n
    qs0, qg0, qc0 = QS.g, QG.g, QC.g

    mm_proj, mm_tb = {}, {}
    mm = mm0
    for i in range(NTILE + 1):
        if i < NTILE:
            mm += 1
            mm_proj[i] = mm
        if i >= 1:
            for bb in range(4):
                mm += 1
                mm_tb[(i - 1, bb)] = mm
    dv_ev, dv_l = {}, {}
    dv = dv0
    for i in range(NTILE + 1):
        if i >= 1:
            for bb in range(4):
                dv += 1
                dv_ev[(i - 1, bb)] = dv
        if i < NTILE:
            for j in range(7):
                dv += 1
                dv_l[(i, j + 1)] = dv
    def act(i):
        return ac0 + 5 * i

    def g_ot(c, h):
        return qs0 + 20 * c + h

    def g_x(i):
        return qs0 + 20 * (i // 4) + 16 + (i % 4)

    def g_xp(i):
        return qc0 + i + (i // 4)

    def g_x2t(c):
        return qc0 + 5 * c + 4

    @ctx.block.gpsimd
    def _(g):
        wsrc = ctx.wo_d[:, :].rearrange("(dc p) n -> p dc n", p=128)
        wdst = wo.rearrange("p (dc n) -> p dc n", dc=16)
        QG.issue(g, wdst, wsrc)

    g_wo = qg0

    @ctx.block.sync
    def _(sync):
        for c in range(NCH):
            seq, tl = c // 4, (c % 4) * 512
            if c >= 2:
                sync.wait_ge(MM.h, mm_proj[4 * (c - 2) + 3])
            for h in range(NH):
                u = seq * NH + h
                QS.issue(sync, ot2[:, (c % 2) * 8192 + h * 512:
                                   (c % 2) * 8192 + (h + 1) * 512],
                         ctx.oT_scr[u, :, tl:tl + 512])
            for j in range(4):
                i = 4 * c + j
                if i >= 2:
                    sync.wait_ge(DV.h, dv_l[(i - 2, 4)])
                assert QS.g == g_x(i)
                QS.issue(sync, xt2[:, (i % 2) * D:(i % 2) * D + D],
                         ctx.x_d[i * 128:(i + 1) * 128, :])

    @ctx.block.tensor
    def _(t):
        for i in range(NTILE + 1):
            if i < NTILE:
                c = i // 4
                QS.wait_many(t, [g_ot(c, h) for h in range(NH)])
                QG.wait(t, g_wo)
                if i >= 1:
                    t.wait_ge(DV.h, dv_l[(i - 1, 1)])
                ot = ot2[:, (c % 2) * 8192:(c % 2) * 8192 + 8192]
                j = i % 4
                for dc in range(NDC):
                    lhs = ot[:, dc * 512 + j * 128:dc * 512 + j * 128 + 128]
                    for nbk in range(4):
                        ins = t.matmul(ctx.psA[:, nbk * 512:(nbk + 1) * 512],
                                       lhs,
                                       wo[:, dc * 2048 + nbk * 512:
                                          dc * 2048 + (nbk + 1) * 512],
                                       start=(dc == 0), stop=(dc == NDC - 1))
                MM.inc(ins)
            if i >= 1:
                ti = i - 1
                t.wait_ge(AC.h, act(ti) + 5)
                for bb in range(4):
                    g = 4 * ti + bb
                    if g >= 2:
                        pb = g - 2
                        t.wait_ge(DV.h, dv_ev[(pb // 4, pb % 4)])
                    reg = (ctx.psB[:, 0:512] if bb % 2 == 0
                           else ctx.psB[:, 512:1024])
                    for j2 in range(4):
                        dc = bb * 4 + j2
                        ins = t.matmul(reg[:, j2 * 128:(j2 + 1) * 128],
                                       xn2bf[:, dc * 128:(dc + 1) * 128],
                                       ctx.eye[:], start=True, stop=True)
                    MM.inc(ins)

    @ctx.block.vector
    def _(v):
        for i in range(NTILE + 1):
            if i >= 1:
                ti = i - 1
                j = ti % 4
                for bb in range(4):
                    v.wait_ge(MM.h, mm_tb[(ti, bb)])
                    if j == 0 and bb == 0 and ti >= 4:
                        QC.wait(v, g_x2t(ti // 4 - 1))
                    reg = (ctx.psB[:, 0:512] if bb % 2 == 0
                           else ctx.psB[:, 512:1024])
                    dst = stg.rearrange("p (dc c) -> p dc c", dc=16)
                    dst = dst[:, bb * 4:(bb + 1) * 4, j * 128:(j + 1) * 128]
                    srcv = reg[:, 0:512].rearrange("p (dc c) -> p dc c", dc=4)
                    ins = v.tensor_copy(dst, srcv)
                    DV.inc(ins)
            if i < NTILE:
                xt = xt2[:, (i % 2) * D:(i % 2) * D + D]
                xpo = xpo2[:, (i % 2) * D:(i % 2) * D + D]
                v.wait_ge(MM.h, mm_proj[i])
                ins = v.scalar_tensor_tensor(ysb, ctx.psA[:, 0:2048], 1.0,
                                             ctx.bo_bc[:], ALU.mult, ALU.add,
                                             accum_out=ctx.sx[:])
                v.drain()
                DV.inc(ins)
                v.wait_ge(AC.h, act(i) + 1)
                v.tensor_scalar(ctx.m1[:], ctx.sx[:], 1.0 / D, None, ALU.mult)
                v.drain()
                v.tensor_mul(ctx.m2[:], ctx.m1[:], ctx.m1[:])
                v.drain()
                ins = v.scalar_tensor_tensor(ctx.var[:], ctx.sxx[:], 1.0 / D,
                                             ctx.m2[:], ALU.mult, ALU.subtract)
                v.drain()
                DV.inc(ins)
                v.wait_ge(AC.h, act(i) + 2)
                ins = v.scalar_tensor_tensor(ctx.nmr[:], ctx.m1[:], -1.0,
                                             ctx.rstd[:], ALU.mult, ALU.mult)
                v.drain()
                DV.inc(ins)
                QS.wait(v, g_x(i))
                ins = v.scalar_tensor_tensor(ysb, ysb, ctx.rstd[:, 0:1], xt,
                                             ALU.mult, ALU.add)
                v.drain()
                DV.inc(ins)
                if i >= 2:
                    QC.wait(v, g_xp(i - 2))
                    v.wait_ge(AC.h, act(i - 2) + 5)
                ins = v.tensor_scalar(xpo, ysb, ctx.nmr[:, 0:1], None, ALU.add,
                                      ALU.add, accum_out=ctx.sx3[:])
                v.drain()
                DV.inc(ins)
                v.wait_ge(AC.h, act(i) + 3)
                v.tensor_scalar(ctx.m13[:], ctx.sx3[:], 1.0 / D, None, ALU.mult)
                v.drain()
                v.tensor_mul(ctx.m2[:], ctx.m13[:], ctx.m13[:])
                v.drain()
                ins = v.scalar_tensor_tensor(ctx.var[:], ctx.sxx3[:], 1.0 / D,
                                             ctx.m2[:], ALU.mult, ALU.subtract)
                v.drain()
                DV.inc(ins)
                v.wait_ge(AC.h, act(i) + 4)
                ins = v.scalar_tensor_tensor(ctx.nmr[:], ctx.m13[:], -1.0,
                                             ctx.rstd[:], ALU.mult, ALU.mult)
                v.drain()
                DV.inc(ins)

    @ctx.block.scalar
    def _(s):
        for i in range(NTILE):
            xpo = xpo2[:, (i % 2) * D:(i % 2) * D + D]
            s.wait_ge(DV.h, dv_l[(i, 1)])
            if i >= 1:
                s.wait_ge(DV.h, dv_l[(i - 1, 2)])
            ins = s.activation(junk, ysb, AF.Square, accum_out=ctx.sxx[:])
            AC.inc(ins)
            s.wait_ge(DV.h, dv_l[(i, 2)])
            s.activation(ctx.tln[:], ctx.var[:], AF.Ln, bias=np.float32(LNEPS))
            s.drain()
            ins = s.activation(ctx.rstd[:], ctx.tln[:], AF.Exp,
                               scale=np.float32(-0.5))
            s.drain()
            AC.inc(ins)
            s.wait_ge(DV.h, dv_l[(i, 5)])
            if i >= 1:
                s.wait_ge(DV.h, dv_l[(i - 1, 6)])
            ins = s.activation(junk, xpo, AF.Square, accum_out=ctx.sxx3[:])
            AC.inc(ins)
            s.wait_ge(DV.h, dv_l[(i, 6)])
            s.activation(ctx.tln[:], ctx.var[:], AF.Ln, bias=np.float32(LNEPS))
            s.drain()
            ins = s.activation(ctx.rstd[:], ctx.tln[:], AF.Exp,
                               scale=np.float32(-0.5))
            s.drain()
            AC.inc(ins)
            s.wait_ge(DV.h, dv_l[(i, 7)])
            if i >= 1:
                s.wait_ge(MM.h, mm_tb[(i - 1, 3)])
            ins = s.activation(xn2bf, xpo, AF.Identity,
                               scale=ctx.rstd[:, 0:1], bias=ctx.nmr[:, 0:1])
            s.drain()
            AC.inc(ins)
            s.wait_ge(DV.h, dv_l[(i, 5)])
            assert QC.g == g_xp(i)
            QC.issue(s, ctx.xp_scr[i], xpo)
            if i % 4 == 3:
                s.wait_ge(DV.h, dv_ev[(i, 3)])
                assert QC.g == g_x2t(i // 4)
                QC.issue(s, ctx.xn2T_scr[i // 4], stg)

    assert MM.n == mm0 + 5 * NTILE
    assert DV.n == dv0 + 11 * NTILE
    assert AC.n == ac0 + 5 * NTILE


def phase_d(ctx):
    MM, DV, AC = ctx.MM, ctx.DV, ctx.AC
    QS, QC = ctx.QS, ctx.QC
    b, f = ctx.bf, ctx.f32

    xn2t2 = b(0, 16384)
    ht = b(16384, 22528)
    ytsb = b(38912, 8192)
    q12 = b(47104, 16384)
    q3b = b(63488, 4096)
    s1b = b(67584, 1024)
    xp2 = f(0, 2 * D)
    out2 = f(2 * D, 2 * D)
    ysb2 = f(4 * D, 2 * D)

    mm0, dv0, ac0 = MM.n, DV.n, AC.n
    qs0, qc0 = QS.g, QC.g

    def mm_x1(c, hc):
        return mm0 + 180 * c + 2 * hc + 1

    def mm_x2(c, hc):
        return mm0 + 180 * c + 2 * hc + 2

    def mm_y(c, half, hc):
        return mm0 + 180 * c + 88 + 44 * half + hc + 1

    def mm_tr(c, tb):
        return mm0 + 180 * c + 176 + tb + 1

    def ac_sig(c, hc):
        return ac0 + 60 * c + hc + 1

    def ac_ye(c, half, k):
        return ac0 + 60 * c + 44 + 4 * half + k + 1

    def ac_sq(c, tb):
        return ac0 + 60 * c + 52 + 2 * tb + 1

    def ac_rstd(c, tb):
        return ac0 + 60 * c + 52 + 2 * tb + 2

    def dv_ht(c, hc):
        return dv0 + 72 * c + hc + 1

    def dv_ye(c, half, k):
        return dv0 + 72 * c + 44 + 4 * half + k + 1

    _EV = {0: 1, 1: 2, 2: 7, 3: 12}
    _CH = {0: 3, 1: 8, 2: 13, 3: 17}

    def dv_ev(c, tb):
        return dv0 + 72 * c + 52 + _EV[tb]

    def dv_ch(c, tb, j):
        return dv0 + 72 * c + 52 + _CH[tb] + j

    def g_xn(c):
        return qs0 + 181 * c

    def g_q1(c, hc):
        return qs0 + 181 * c + 2 * hc + 1

    def g_q2(c, hc):
        return qs0 + 181 * c + 2 * hc + 2

    def g_q3(c, half, hc):
        return qs0 + 181 * c + 89 + 44 * half + hc

    def g_xpl(c, tb):
        return qs0 + 181 * c + 177 + tb

    def g_out(c, tb):
        return qc0 + 4 * c + tb

    @ctx.block.sync
    def _(sync):
        for c in range(NCH):
            if c >= 2:
                sync.wait_ge(MM.h, mm_x2(c - 2, NHC - 1))
            assert QS.g == g_xn(c)
            QS.issue(sync, xn2t2[:, (c % 2) * 8192:(c % 2) * 8192 + 8192],
                     ctx.xn2T_scr[c])
            for hc in range(NHC):
                k = c * NHC + hc
                if k >= 4:
                    pk = k - 4
                    pc, ph = pk // NHC, pk % NHC
                    sync.wait_ge(MM.h, mm_x1(pc, ph))
                QS.issue(sync, q12[:, (k % 4) * 2048:(k % 4) * 2048 + 2048],
                         ctx.q1T_scr[hc])
                if k >= 4:
                    pk = k - 4
                    pc, ph = pk // NHC, pk % NHC
                    sync.wait_ge(MM.h, mm_x2(pc, ph))
                QS.issue(sync, q12[:, 8192 + (k % 4) * 2048:
                                   8192 + (k % 4) * 2048 + 2048],
                         ctx.q2T_scr[hc])
            for half in range(2):
                for hc in range(NHC):
                    k = c * 2 * NHC + half * NHC + hc
                    if k >= 4:
                        kk = k - 4
                        pc = kk // (2 * NHC)
                        ph, phc = (kk % (2 * NHC)) // NHC, kk % NHC
                        sync.wait_ge(MM.h, mm_y(pc, ph, phc))
                    wsrc = ctx.q3n_scr[half * 1024:(half + 1) * 1024,
                                       hc * 128:(hc + 1) * 128]
                    assert QS.g == g_q3(c, half, hc)
                    QS.issue(sync, q3b[:, (k % 4) * 1024:(k % 4) * 1024 + 1024],
                             wsrc, transpose=True)
            for tb in range(4):
                g = 4 * c + tb
                if g >= 2:
                    sync.wait_ge(DV.h, dv_ch((g - 2) // 4, (g - 2) % 4, 3))
                assert QS.g == g_xpl(c, tb)
                QS.issue(sync, xp2[:, (tb % 2) * D:(tb % 2) * D + D],
                         ctx.xp_scr[g])

    @ctx.block.tensor
    def _(t):
        for c in range(NCH):
            xn = xn2t2[:, (c % 2) * 8192:(c % 2) * 8192 + 8192]
            QS.wait(t, g_xn(c))
            for hc in range(NHC):
                k = c * NHC + hc
                QS.wait(t, g_q1(c, hc))
                if c >= 1 and hc == 0:
                    t.wait_ge(DV.h, dv_ev(c - 1, 2))
                if k >= 2:
                    pc, ph = (c, hc - 2) if hc >= 2 else (c - 1, NHC - 2 + hc)
                    t.wait_ge(AC.h, ac_sig(pc, ph))
                q1 = q12[:, (k % 4) * 2048:(k % 4) * 2048 + 2048]
                reg1 = ctx.psA[:, (hc % 2) * 512:(hc % 2) * 512 + 512]
                for dc in range(NDC):
                    ins = t.matmul(reg1, q1[:, dc * 128:(dc + 1) * 128],
                                   xn[:, dc * 512:(dc + 1) * 512],
                                   start=(dc == 0), stop=(dc == NDC - 1))
                MM.inc(ins)
                QS.wait(t, g_q2(c, hc))
                if k >= 2:
                    pc, ph = (c, hc - 2) if hc >= 2 else (c - 1, NHC - 2 + hc)
                    t.wait_ge(DV.h, dv_ht(pc, ph))
                q2 = q12[:, 8192 + (k % 4) * 2048:8192 + (k % 4) * 2048 + 2048]
                reg2 = ctx.psA[:, 1024 + (hc % 2) * 512:
                               1024 + (hc % 2) * 512 + 512]
                for dc in range(NDC):
                    ins = t.matmul(reg2, q2[:, dc * 128:(dc + 1) * 128],
                                   xn[:, dc * 512:(dc + 1) * 512],
                                   start=(dc == 0), stop=(dc == NDC - 1))
                MM.inc(ins)
            for half in range(2):
                if half == 0:
                    t.wait_ge(AC.h, ac_sig(c, NHC - 1))
                    t.wait_ge(DV.h, dv_ht(c, NHC - 1))
                    if c >= 1:
                        t.wait_ge(DV.h, dv_ev(c - 1, 3))
                for hc in range(NHC):
                    k = half * NHC + hc
                    QS.wait(t, g_q3(c, half, hc))
                    kg = (c * 2 * NHC + k) % 4
                    q3 = q3b[:, kg * 1024:kg * 1024 + 1024]
                    for dcl in range(8):
                        if half == 1 and hc == 0:
                            if dcl < 4:
                                t.wait_ge(DV.h, dv_ye(c, 0, dcl))
                            else:
                                t.wait_ge(AC.h, ac_ye(c, 0, dcl - 4))
                        reg = ctx.psA if dcl < 4 else ctx.psB
                        ins = t.matmul(
                            reg[:, (dcl % 4) * 512:(dcl % 4) * 512 + 512],
                            q3[:, dcl * 128:(dcl + 1) * 128],
                            ht[:, hc * 512:(hc + 1) * 512],
                            start=(hc == 0), stop=(hc == NHC - 1))
                    MM.inc(ins)
            for tb in range(4):
                if tb == 0:
                    t.wait_ge(DV.h, dv_ye(c, 1, 3))
                    t.wait_ge(AC.h, ac_ye(c, 1, 3))
                elif tb >= 2:
                    t.wait_ge(DV.h, dv_ev(c, tb - 2))
                reg = ctx.psA if tb % 2 == 0 else ctx.psB
                for dc in range(NDC):
                    lhs = ytsb[:, dc * 512 + tb * 128:dc * 512 + tb * 128 + 128]
                    ins = t.matmul(reg[:, dc * 128:(dc + 1) * 128], lhs,
                                   ctx.eye[:], start=True, stop=True)
                MM.inc(ins)

    @ctx.block.scalar
    def _(s):
        for c in range(NCH):
            for hc in range(NHC):
                k = c * NHC + hc
                s.wait_ge(MM.h, mm_x1(c, hc))
                if k >= 2:
                    pc, ph = (c, hc - 2) if hc >= 2 else (c - 1, NHC - 2 + hc)
                    s.wait_ge(DV.h, dv_ht(pc, ph))
                reg1 = ctx.psA[:, (hc % 2) * 512:(hc % 2) * 512 + 512]
                ins = s.activation(s1b[:, (hc % 2) * 512:(hc % 2) * 512 + 512],
                                   reg1, AF.Sigmoid)
                AC.inc(ins)
            for half in range(2):
                s.wait_ge(MM.h, mm_y(c, half, NHC - 1))
                for k in range(4):
                    dc = half * 8 + 4 + k
                    ins = s.copy(ytsb[:, dc * 512:dc * 512 + 512],
                                 ctx.psB[:, k * 512:k * 512 + 512])
                    AC.inc(ins)
            for tb in range(4):
                g = 4 * c + tb
                ysb = ysb2[:, (tb % 2) * D:(tb % 2) * D + D]
                out = out2[:, (tb % 2) * D:(tb % 2) * D + D]
                if g >= 2:
                    QC.wait(s, g_out((g - 2) // 4, (g - 2) % 4))
                s.wait_ge(DV.h, dv_ev(c, tb))
                if (c, tb) != (0, 0):
                    pc, pt = (c, tb - 1) if tb > 0 else (c - 1, 3)
                    s.wait_ge(DV.h, dv_ch(pc, pt, 0))
                ins = s.activation(out, ysb, AF.Square, accum_out=ctx.sxx[:])
                AC.inc(ins)
                s.wait_ge(DV.h, dv_ch(c, tb, 0))
                s.activation(ctx.tln[:], ctx.var[:], AF.Ln, bias=np.float32(LNEPS))
                s.drain()
                ins = s.activation(ctx.rstd[:], ctx.tln[:], AF.Exp,
                               scale=np.float32(-0.5))
                s.drain()
                AC.inc(ins)
                s.wait_ge(DV.h, dv_ch(c, tb, 3))
                assert QC.g == g_out(c, tb)
                QC.issue(s, ctx.out_d[(c * 4 + tb) * 128:
                                      (c * 4 + tb) * 128 + 128, :], out)

    @ctx.block.vector
    def _(v):
        for c in range(NCH):
            for hc in range(NHC):
                v.wait_ge(MM.h, mm_x2(c, hc))
                v.wait_ge(AC.h, ac_sig(c, hc))
                if c >= 1 and hc == 0:
                    v.wait_ge(MM.h, mm_y(c - 1, 1, NHC - 1))
                reg2 = ctx.psA[:, 1024 + (hc % 2) * 512:
                               1024 + (hc % 2) * 512 + 512]
                ins = v.tensor_tensor(ht[:, hc * 512:(hc + 1) * 512], reg2,
                                      s1b[:, (hc % 2) * 512:(hc % 2) * 512 + 512],
                                      ALU.mult)
                DV.inc(ins)
            for half in range(2):
                v.wait_ge(MM.h, mm_y(c, half, NHC - 1))
                for k in range(4):
                    dc = half * 8 + k
                    ins = v.tensor_copy(
                        ytsb[:, dc * 512:(dc + 1) * 512],
                        ctx.psA[:, k * 512:k * 512 + 512])
                    DV.inc(ins)

            def ev(tb, c=c, v=v):
                v.wait_ge(MM.h, mm_tr(c, tb))
                if tb >= 2:
                    v.wait_ge(AC.h, ac_sq(c, tb - 2))
                reg = ctx.psA if tb % 2 == 0 else ctx.psB
                ysb = ysb2[:, (tb % 2) * D:(tb % 2) * D + D]
                ins = v.tensor_scalar(ysb, reg[:, 0:2048], 1.0, None,
                                      ALU.mult, ALU.add,
                                      accum_out=ctx.sx2[:, tb % 2:tb % 2 + 1])
                v.drain()
                DV.inc(ins)

            def ch(tb, c=c, v=v):
                v.wait_ge(AC.h, ac_sq(c, tb))
                v.tensor_scalar(ctx.m1[:], ctx.sx2[:, tb % 2:tb % 2 + 1],
                                1.0 / D, None, ALU.mult)
                v.drain()
                v.tensor_mul(ctx.m2[:], ctx.m1[:], ctx.m1[:])
                v.drain()
                ins = v.scalar_tensor_tensor(ctx.var[:], ctx.sxx[:], 1.0 / D,
                                             ctx.m2[:], ALU.mult, ALU.subtract)
                v.drain()
                DV.inc(ins)
                v.wait_ge(AC.h, ac_rstd(c, tb))
                ins = v.scalar_tensor_tensor(ctx.nmr[:], ctx.m1[:], -1.0,
                                             ctx.rstd[:], ALU.mult, ALU.mult)
                v.drain()
                DV.inc(ins)
                QS.wait(v, g_xpl(c, tb))
                g = 4 * c + tb
                if g >= 2:
                    QC.wait(v, g_out((g - 2) // 4, (g - 2) % 4))
                ysb = ysb2[:, (tb % 2) * D:(tb % 2) * D + D]
                out = out2[:, (tb % 2) * D:(tb % 2) * D + D]
                ins = v.scalar_tensor_tensor(out, ysb, ctx.rstd[:, 0:1],
                                             xp2[:, (tb % 2) * D:(tb % 2) * D + D],
                                             ALU.mult, ALU.add)
                v.drain()
                DV.inc(ins)
                ins = v.tensor_scalar(out, out, ctx.nmr[:, 0:1], None, ALU.add)
                v.drain()
                DV.inc(ins)

            ev(0); ev(1); ch(0); ev(2); ch(1); ev(3); ch(2); ch(3)

    @ctx.block.sync
    def _(sync):
        QC.wait_all(sync)

    assert MM.n == mm0 + 180 * NCH
    assert AC.n == ac0 + 60 * NCH
    assert DV.n == dv0 + 72 * NCH


def build():
    nc = bass.Bass("TRN2", target_bir_lowering=False)
    ctx = Ctx()
    ctx.nc = nc

    ctx.x_d = nc.dram_tensor("x", [NT, D], F32, kind="ExternalInput").ap()
    ctx.wq_d = nc.dram_tensor("wq", [D, D], F32, kind="ExternalInput").ap()
    ctx.wk_d = nc.dram_tensor("wk", [D, D], F32, kind="ExternalInput").ap()
    ctx.wv_d = nc.dram_tensor("wv", [D, D], F32, kind="ExternalInput").ap()
    ctx.wo_d = nc.dram_tensor("wo", [D, D], F32, kind="ExternalInput").ap()
    ctx.bq_d = nc.dram_tensor("bq", [1, D], F32, kind="ExternalInput").ap()
    ctx.bk_d = nc.dram_tensor("bk", [1, D], F32, kind="ExternalInput").ap()
    ctx.bv_d = nc.dram_tensor("bv", [1, D], F32, kind="ExternalInput").ap()
    ctx.bo_d = nc.dram_tensor("bo", [1, D], F32, kind="ExternalInput").ap()
    ctx.w1_d = nc.dram_tensor("w1", [HF, D], F32, kind="ExternalInput").ap()
    ctx.w2_d = nc.dram_tensor("w2", [HF, D], F32, kind="ExternalInput").ap()
    ctx.w3_d = nc.dram_tensor("w3", [D, HF], F32, kind="ExternalInput").ap()
    ctx.out_d = nc.dram_tensor("out", [NT, D], F32, kind="ExternalOutput").ap()

    ctx.xnT_scr = nc.dram_tensor("xnT_scr", [NCH, 128, 8192], BF16).ap()
    ctx.q_scr = nc.dram_tensor("q_scr", [NUNIT, 128, T], BF16).ap()
    ctx.k_scr = nc.dram_tensor("k_scr", [NUNIT, 128, T], BF16).ap()
    ctx.v_scr = nc.dram_tensor("v_scr", [NUNIT, 128, T], BF16).ap()
    ctx.oT_scr = nc.dram_tensor("oT_scr", [NUNIT, 128, T], BF16).ap()
    ctx.xp_scr = nc.dram_tensor("xp_scr", [NTILE, 128, D], F32).ap()
    ctx.xn2T_scr = nc.dram_tensor("xn2T_scr", [NCH, 128, 8192], BF16).ap()
    ctx.q1T_scr = nc.dram_tensor("q1T_scr", [NHC, 128, 2048], BF16).ap()
    ctx.q2T_scr = nc.dram_tensor("q2T_scr", [NHC, 128, 2048], BF16).ap()
    ctx.q3n_scr = nc.dram_tensor("q3n_scr", [D, HF], BF16).ap()

    with contextlib.ExitStack() as stack:
        ctx.ARB = stack.enter_context(
            nc.sbuf_tensor("arena_bf", [128, 69632], BF16))
        ctx.ARF = stack.enter_context(
            nc.sbuf_tensor("arena_f32", [128, 12544], F32))

        def bf(off, n):
            return ctx.ARB[:, off:off + n]

        def f32(off, n):
            return ctx.ARF[:, off:off + n]

        ctx.bf = bf
        ctx.f32 = f32

        def sb(name, shape, dt):
            t = stack.enter_context(nc.sbuf_tensor(name, shape, dt))
            setattr(ctx, name, t)

        sb("eye", [128, 128], BF16)
        sb("tril", [128, 128], BF16)
        sb("mshuf", [128, 128], BF16)
        sb("supe", [128, 128], BF16)
        sb("subo", [128, 128], BF16)
        sb("onesb", [128, 128], BF16)
        sb("erow", [1, 128], F32)
        sb("invf", [1, 128], F32)
        sb("onesrow", [1, 128], F32)
        sb("onescol", [128, 1], F32)
        sb("one11", [1, 1], F32)
        sb("bq_t", [128, 16], F32)
        sb("bk_t", [128, 16], F32)
        sb("bv_t", [128, 16], F32)
        sb("bo_bc", [128, D], F32)
        sb("gcols", [128, 48], F32)
        sb("gtot", [128, 1], F32)
        sb("thr11", [1, 1], F32)
        sb("thrbc", [128, 1], F32)
        sb("nthrbc", [128, 1], F32)
        sb("sx", [128, 1], F32)
        sb("sx2", [128, 2], F32)
        sb("sxx", [128, 1], F32)
        sb("sx3", [128, 1], F32)
        sb("sxx3", [128, 1], F32)
        sb("m1", [128, 1], F32)
        sb("m13", [128, 1], F32)
        sb("m2", [128, 1], F32)
        sb("var", [128, 1], F32)
        sb("tln", [128, 1], F32)
        sb("rstd", [128, 1], F32)
        sb("nmr", [128, 1], F32)
        sb("accA2", [128, 2], F32)
        sb("accB2", [128, 2], F32)
        sb("dsum", [128, 1], F32)
        sb("dinv", [128, 1], F32)
        sb("dq2", [128, 256], BF16)

        ctx.psA = stack.enter_context(nc.psum_tensor("psA", [128, 2048], F32))
        ctx.psB = stack.enter_context(nc.psum_tensor("psB", [128, 2048], F32))

        ctx.MM = Ctr(stack, nc, "MM")
        ctx.DV = Ctr(stack, nc, "DV")
        ctx.AC = Ctr(stack, nc, "AC")
        ctx.GP = Ctr(stack, nc, "GP")
        ctx.all_ctrs = [ctx.MM, ctx.DV, ctx.AC, ctx.GP]
        ctx.QS = DmaQ(stack, nc, "QS", 6)
        ctx.QG = DmaQ(stack, nc, "QG", 3)
        ctx.QC = DmaQ(stack, nc, "QC", 4)
        ctx.QV = DmaQ(stack, nc, "QV", 3)

        with nc.Block() as block:
            ctx.block = block

            if "prep" in BUILD_PHASES:
                phase_prep(ctx)
                _barrier(ctx)
            if "quant" in BUILD_PHASES:
                phase_quant(ctx)
                _barrier(ctx)
            if "a0" in BUILD_PHASES:
                phase_a0(ctx)
                _barrier(ctx)
            for st in ("aq", "ak", "av"):
                if st in BUILD_PHASES:
                    phase_stage(ctx, st[1])
                    _barrier(ctx)
            if "b" in BUILD_PHASES:
                phase_b(ctx)
                _barrier(ctx)
            if "c" in BUILD_PHASES:
                phase_c(ctx)
                _barrier(ctx)
            if "d" in BUILD_PHASES:
                phase_d(ctx)
            else:
                _barrier(ctx)

    return nc


def kernel(L0_attn_x, L0_attn_Wq, L0_attn_bq, L0_attn_Wk, L0_attn_bk,
           L0_attn_Wv, L0_attn_bv, L0_attn_Wo, L0_attn_bo,
           L0_ffn_w1, L0_ffn_w2, L0_ffn_w3):
    x = np.ascontiguousarray(np.asarray(L0_attn_x, dtype=np.float32))
    args = {
        "wq": np.ascontiguousarray(np.asarray(L0_attn_Wq, np.float32)),
        "wk": np.ascontiguousarray(np.asarray(L0_attn_Wk, np.float32)),
        "wv": np.ascontiguousarray(np.asarray(L0_attn_Wv, np.float32)),
        "wo": np.ascontiguousarray(np.asarray(L0_attn_Wo, np.float32)),
        "bq": np.ascontiguousarray(np.asarray(L0_attn_bq, np.float32).reshape(1, D)),
        "bk": np.ascontiguousarray(np.asarray(L0_attn_bk, np.float32).reshape(1, D)),
        "bv": np.ascontiguousarray(np.asarray(L0_attn_bv, np.float32).reshape(1, D)),
        "bo": np.ascontiguousarray(np.asarray(L0_attn_bo, np.float32).reshape(1, D)),
        "w1": np.ascontiguousarray(np.asarray(L0_ffn_w1, np.float32)),
        "w2": np.ascontiguousarray(np.asarray(L0_ffn_w2, np.float32)),
        "w3": np.ascontiguousarray(np.asarray(L0_ffn_w3, np.float32)),
    }
    nc = build()
    in_maps = []
    for c in range(8):
        shard = np.ascontiguousarray(x[2 * c:2 * c + 2].reshape(NT, D))
        m = {"x": shard}
        m.update(args)
        in_maps.append(m)
    res = run_bass_kernel_spmd(nc, in_maps, core_ids=list(range(8)))
    out = np.concatenate(
        [np.asarray(r["out"]).reshape(NSEQ, T, D) for r in res.results], axis=0)
    return out
